# version 13 (final)
# baseline (speedup 1.0000x reference)
import numpy as np
from contextlib import ExitStack

import concourse.bass as bass
import concourse.mybir as mybir
from concourse.bass_utils import run_bass_kernel_spmd
from concourse.aot_env import allow_ici_all_to_all

F32 = mybir.dt.float32
BF16 = mybir.dt.bfloat16
AF = mybir.ActivationFunctionType
ALU = mybir.AluOpType
AX = mybir.AxisListType

D = 1024
DFF = 4096
NEXP = 9
GRP = 512
QT = 512
EXP_SHIFT = -4.0


class Sem:
    def __init__(self, nc, stack, name):
        self.h = stack.enter_context(nc.semaphore(name))
        self.n = 0

    def tick(self, k=1):
        self.n += k
        return self.n


class Prog:
    ENGS = ("sync", "gpsimd", "tensor", "vector", "scalar")

    def __init__(self):
        self.ev = {e: [] for e in self.ENGS}

    def on(self, eng, fn):
        self.ev[eng].append(fn)

    def emit(self, nc):
        with nc.Block() as block:
            for name in self.ENGS:
                fns = self.ev[name]
                if not fns:
                    continue

                def run(engine, fns=fns):
                    for f in fns:
                        f(engine)

                getattr(block, name)(run)


def build_program(seq=8192, n_cores=8, debug=False):
    S = seq
    NT = 2 * S // n_cores
    G = 2 * S // GRP
    NQT = S // QT
    NKT = S // 128
    NTQ = NT // QT
    NTT = NT // 128
    NX4 = NT // GRP
    PAIRS_HB = NQT * (NQT + 1)

    nc = bass.Bass("TRN2", target_bir_lowering=False)

    x_full = nc.dram_tensor("x_full", [2, S, D], F32, kind="ExternalInput").ap()
    x_mine = nc.dram_tensor("x_mine", [NT, D], F32, kind="ExternalInput").ap()
    wq_sl = nc.dram_tensor("wq_sl", [D, 128], F32, kind="ExternalInput").ap()
    wk_sl = nc.dram_tensor("wk_sl", [D, 128], F32, kind="ExternalInput").ap()
    wv_sl = nc.dram_tensor("wv_sl", [D, 128], F32, kind="ExternalInput").ap()
    wo_t = nc.dram_tensor("wo_t", [D, D], F32, kind="ExternalInput").ap()
    rt_in = nc.dram_tensor("rt", [D, 8], F32, kind="ExternalInput").ap()
    wfc_t = nc.dram_tensor("wfc_t", [NEXP, D, DFF], F32, kind="ExternalInput").ap()
    wpr_t = nc.dram_tensor("wpr_t", [NEXP, DFF, D], F32, kind="ExternalInput").ap()
    cos_t = nc.dram_tensor("cos_t", [64, S], F32, kind="ExternalInput").ap()
    sin_t = nc.dram_tensor("sin_t", [64, S], F32, kind="ExternalInput").ap()
    masks_in = nc.dram_tensor("masks", [4, 128, QT], F32, kind="ExternalInput").ap()
    oh_in = nc.dram_tensor("onehot", [NQT, NQT * 64], F32, kind="ExternalInput").ap()
    ident_in = nc.dram_tensor("ident", [128, 128], F32, kind="ExternalInput").ap()
    ones_in = nc.dram_tensor("ones_r", [1, 128], F32, kind="ExternalInput").ap()
    out_mine = nc.dram_tensor("out_mine", [NT, D], F32, kind="ExternalOutput").ap()

    dkind = "ExternalOutput" if debug else "Internal"
    kscr = nc.dram_tensor("kscr", [2, 2, 64, S], BF16, kind=dkind).ap()
    qscr = nc.dram_tensor("qscr", [2, 2, 64, S], BF16, kind=dkind).ap()
    og_l = nc.dram_tensor("og_l", [n_cores * 128, NT], BF16).ap()
    og_m = nc.dram_tensor("og_m", [n_cores * 128, NT], BF16).ap()
    if debug:
        dbg_acc = nc.dram_tensor("dbg_acc", [128, 8 * NT], F32, kind="ExternalOutput").ap()
        dbg_gsb = nc.dram_tensor("dbg_gsb", [8, NT], F32, kind="ExternalOutput").ap()
        dbg_xnq = nc.dram_tensor("dbg_xnq", [128, 8 * NT], BF16, kind="ExternalOutput").ap()
        dbg_ogm = nc.dram_tensor("dbg_ogm", [n_cores * 128, NT], BF16, kind="ExternalOutput").ap()

    with ExitStack() as st:
        E = st.enter_context

        identb = E(nc.sbuf_tensor("identb", [128, 128], BF16))
        identf = E(nc.sbuf_tensor("identf", [128, 128], F32))
        ones_f = E(nc.sbuf_tensor("ones_f", [1, 128], F32))
        ebias = E(nc.sbuf_tensor("ebias", [128, 1], F32))

        SM = {}
        for name in ["b0", "xin", "cs", "sq", "vsq", "vstat", "sln", "vnb", "xn",
                     "ttr", "xnTe", "kvq", "rope", "spill", "v4c", "kq",
                     "smm", "exp", "msk", "pv", "lr", "lsh", "lrp", "lcp", "lmv", "kz",
                     "ot", "og", "cc", "otf", "wocast", "rtc", "womm",
                     "woev", "lgt", "rtv", "rte", "gG", "gtr", "gev",
                     "grep", "gsq", "grp", "gcp", "xne", "wfcd", "hmm",
                     "relu", "sqd", "sqv", "wprd", "ymm", "yev", "ept", "epx",
                     "epv", "out"]:
            SM[name] = Sem(nc, st, "s_" + name)

        r1 = ExitStack()
        with r1:
            E1 = r1.enter_context
            V4 = E1(nc.sbuf_tensor("V4", [128, 4 * NKT * 65], BF16))
            masks_sb = E1(nc.sbuf_tensor("masks_sb", [128, 4 * QT], BF16))
            wqsb = E1(nc.sbuf_tensor("wqsb", [128, 8 * 128], BF16))
            wksb = E1(nc.sbuf_tensor("wksb", [128, 8 * 128], BF16))
            wvsb = E1(nc.sbuf_tensor("wvsb", [128, 8 * 128], BF16))
            oh16 = E1(nc.sbuf_tensor("oh16", [NQT, NQT * 64], F32))

            p0 = Prog()

            def b0g(g):
                b0h = SM["b0"].h
                g.dma_start(out=identb[:, :], in_=ident_in).then_inc(b0h, 16)
                g.dma_start(out=identf[:, :], in_=ident_in).then_inc(b0h, 16)
                g.dma_start(out=ones_f[:, :], in_=ones_in).then_inc(b0h, 16)
                g.dma_start(out=masks_sb[:, :].rearrange("p (m s) -> p m s", m=4),
                            in_=masks_in.rearrange("m p s -> p m s")).then_inc(b0h, 16)
                g.dma_start(out=wqsb[:, :].rearrange("p (k m) -> p k m", k=8),
                            in_=wq_sl.rearrange("(k p) m -> p k m", p=128)).then_inc(b0h, 16)
                g.dma_start(out=wksb[:, :].rearrange("p (k m) -> p k m", k=8),
                            in_=wk_sl.rearrange("(k p) m -> p k m", p=128)).then_inc(b0h, 16)
                g.dma_start(out=wvsb[:, :].rearrange("p (k m) -> p k m", k=8),
                            in_=wv_sl.rearrange("(k p) m -> p k m", p=128)).then_inc(b0h, 16)
                g.dma_start(out=oh16[:, :], in_=oh_in).then_inc(b0h, 16)
                g.wait_ge(b0h, 128)

            def b0v(v):
                v.memset(ebias[:, :], EXP_SHIFT)
                for hb in range(4):
                    v.memset(
                        V4[:, hb * NKT * 65:(hb + 1) * NKT * 65]
                        .rearrange("p (k s) -> p k s", k=NKT)[:, :, 64:65], 1.0)

            p0.on("gpsimd", b0g)
            p0.on("vector", b0v)
            p0.emit(nc)

            with ExitStack() as rb1:
                E2 = rb1.enter_context
                xg = [E2(nc.sbuf_tensor(f"xg{i}", [128, 4 * D], F32)) for i in range(4)]
                xnT = [E2(nc.sbuf_tensor(f"xnT{i}", [128, 8 * GRP], BF16)) for i in range(3)]
                xn_sb = [E2(nc.sbuf_tensor(f"xnsb{i}", [128, D], BF16)) for i in range(4)]
                xsq = [E2(nc.sbuf_tensor(f"xsq{i}", [128, D], F32)) for i in range(4)]
                stats = E2(nc.sbuf_tensor("stats", [128, 64], F32))
                kqstage = [E2(nc.sbuf_tensor(f"kqs{i}", [64, 2 * GRP], BF16)) for i in range(3)]
                qqstage = [E2(nc.sbuf_tensor(f"qqs{i}", [64, 2 * GRP], BF16)) for i in range(3)]
                rpt = [E2(nc.sbuf_tensor(f"rpt{i}", [64, GRP], F32)) for i in range(4)]
                ctl = [E2(nc.sbuf_tensor(f"ctl{i}", [64, GRP], F32)) for i in range(4)]
                stl = [E2(nc.sbuf_tensor(f"stl{i}", [64, GRP], F32)) for i in range(4)]
                xps = [E2(nc.psum_tensor(f"xps{i}", [128, 8 * 128], BF16)) for i in range(3)]
                kps1 = E2(nc.psum_tensor("kps1", [64, GRP], F32))
                kps2 = E2(nc.psum_tensor("kps2", [64, GRP], F32))
                qps1 = E2(nc.psum_tensor("qps1", [64, GRP], F32))
                qps2 = E2(nc.psum_tensor("qps2", [64, GRP], F32))
                vps = E2(nc.psum_tensor("vps", [128, GRP], F32))

                p1 = Prog()
                for g in range(G):
                    bt = g // (G // 2)
                    pos0 = (g % (G // 2)) * GRP
                    p = g % 3
                    p4b = g % 4

                    w_xn = 4 * (g - 3)
                    w_rp = 2 * (g - 3)
                    x_t = SM["xin"].tick(16)
                    c_t = SM["cs"].tick(32)

                    def ld(s, g=g, bt=bt, pos0=pos0, p4b=p4b, w_xn=w_xn):
                        if g >= 4:
                            s.wait_ge(SM["xn"].h, w_xn)
                        s.dma_start(
                            out=xg[p4b][:, :].rearrange("p (t d) -> p t d", t=4),
                            in_=x_full[bt, pos0:pos0 + GRP, :]
                            .rearrange("(t p) d -> p t d", p=128)
                        ).then_inc(SM["xin"].h, 16)
                    p1.on("sync", ld)

                    def ldcs(g_, g=g, bt=bt, pos0=pos0, p4b=p4b, w_rp=w_rp):
                        if g >= 4:
                            g_.wait_ge(SM["rope"].h, w_rp)
                        g_.dma_start(out=ctl[p4b][:, :], in_=cos_t[:, pos0:pos0 + GRP]
                                     ).then_inc(SM["cs"].h, 16)
                        g_.dma_start(out=stl[p4b][:, :], in_=sin_t[:, pos0:pos0 + GRP]
                                     ).then_inc(SM["cs"].h, 16)
                    p1.on("gpsimd", ldcs)

                    for t in range(4):
                        emit_ln_sq(p1, SM, xg[p4b], t, 4 * g + t, xsq, x_t,
                                   stats)
                    for tp in range(2):
                        emit_ln_stats_pair(p1, SM, xg[p4b], 2 * tp, 2 * tp + 1,
                                           4 * g + 2 * tp, 4 * g + 2 * tp + 1,
                                           xsq, stats, x_t)
                    nbts = []
                    for t in range(4):
                        emit_ln_nb(p1, SM, 4 * g + t, stats)
                        nbts.append(SM["vnb"].n)
                    for t in range(4):
                        i = 4 * g + t
                        emit_ln_fin(p1, SM, xg[p4b], t, i, stats, xn_sb, nbts[t])
                    for t in range(4):
                        i = 4 * g + t
                        emit_tr_evac(p1, SM, xn_sb, xps, identb, i, xnT[p], t * 128,
                                     kvq_guard=(3 * (g - 2) if (t == 0 and g >= 3) else None))

                    w_xnTe = 4 * g + 4
                    kvq_t = SM["kvq"].tick()

                    def te_k(te, p=p, g=g, w1=w_xnTe, w2=2 * g - 1):
                        te.wait_ge(SM["xnTe"].h, w1)
                        if g >= 1:
                            te.wait_ge(SM["rope"].h, w2)
                        for half, dst in ((0, kps1), (1, kps2)):
                            for k in range(8):
                                ins = te.matmul(
                                    dst[:, :],
                                    wksb[:, k * 128 + half * 64:k * 128 + half * 64 + 64],
                                    xnT[p][:, k * GRP:(k + 1) * GRP],
                                    start=(k == 0), stop=(k == 7),
                                    skip_group_check=True)
                        ins.then_inc(SM["kvq"].h, 1)
                    p1.on("tensor", te_k)

                    w_spK = 128 * (g - 3) + 64
                    rope_tK = SM["rope"].tick()

                    def ve_rk(v, p=p, p4b=p4b, g=g, w1=kvq_t, w2=c_t, w3=w_spK):
                        v.wait_ge(SM["kvq"].h, w1)
                        v.wait_ge(SM["cs"].h, w2)
                        if g >= 3:
                            v.wait_ge(SM["spill"].h, w3)
                        emit_rope(v, kps1, kps2, ctl[p4b], stl[p4b], kqstage[p], rpt,
                                  SM["rope"].h)
                    p1.on("vector", ve_rk)

                    SM["spill"].tick(64)

                    def sy_spk(s, p=p, bt=bt, pos0=pos0, w1=rope_tK):
                        s.wait_ge(SM["rope"].h, w1)
                        for h in range(2):
                            s.dma_start(out=kscr[h, bt, 0:32, pos0:pos0 + GRP],
                                        in_=kqstage[p][h * 32:(h + 1) * 32, 0:GRP]
                                        ).then_inc(SM["spill"].h, 16)
                            s.dma_start(out=kscr[h, bt, 32:64, pos0:pos0 + GRP],
                                        in_=kqstage[p][h * 32:(h + 1) * 32, GRP:2 * GRP]
                                        ).then_inc(SM["spill"].h, 16)
                    p1.on("gpsimd", sy_spk)

                    w_v4 = 8 * g
                    kvq_tv = SM["kvq"].tick()

                    def te_v(te, p=p, g=g, w1=w_v4):
                        if g >= 1:
                            te.wait_ge(SM["v4c"].h, w1)
                        for t4 in range(4):
                            for k in range(8):
                                ins = te.matmul(
                                    vps[:, t4 * 128:(t4 + 1) * 128],
                                    xnT[p][:, k * GRP + t4 * 128:k * GRP + (t4 + 1) * 128],
                                    wvsb[:, k * 128:(k + 1) * 128],
                                    start=(k == 0), stop=(k == 7),
                                    skip_group_check=True)
                        ins.then_inc(SM["kvq"].h, 1)
                    p1.on("tensor", te_v)

                    SM["v4c"].tick(8)

                    def sc_v4(s, g=g, bt=bt, pos0=pos0, w1=kvq_tv):
                        s.wait_ge(SM["kvq"].h, w1)
                        for t4 in range(4):
                            kt = pos0 // 128 + t4
                            for h in range(2):
                                base = (h * 2 + bt) * NKT * 65 + kt * 65
                                ins = s.copy(
                                    V4[:, base:base + 64],
                                    vps[:, t4 * 128 + h * 64:t4 * 128 + h * 64 + 64])
                        ins.then_inc(SM["v4c"].h, 8)
                    p1.on("scalar", sc_v4)

                    kvq_tq = SM["kvq"].tick()

                    def te_q(te, p=p, g=g, w1=2 * g):
                        if g >= 1:
                            te.wait_ge(SM["rope"].h, w1)
                        for half, dst in ((0, qps1), (1, qps2)):
                            for k in range(8):
                                ins = te.matmul(
                                    dst[:, :],
                                    wqsb[:, k * 128 + half * 64:k * 128 + half * 64 + 64],
                                    xnT[p][:, k * GRP:(k + 1) * GRP],
                                    start=(k == 0), stop=(k == 7),
                                    skip_group_check=True)
                        ins.then_inc(SM["kvq"].h, 1)
                    p1.on("tensor", te_q)

                    w_spQ = 128 * (g - 2)
                    rope_tQ = SM["rope"].tick()

                    def ve_rq(v, p=p, p4b=p4b, g=g, w1=kvq_tq, w3=w_spQ):
                        v.wait_ge(SM["kvq"].h, w1)
                        if g >= 3:
                            v.wait_ge(SM["spill"].h, w3)
                        emit_rope(v, qps1, qps2, ctl[p4b], stl[p4b], qqstage[p], rpt,
                                  SM["rope"].h)
                    p1.on("vector", ve_rq)

                    SM["spill"].tick(64)

                    def sy_spq(s, p=p, bt=bt, pos0=pos0, w1=rope_tQ):
                        s.wait_ge(SM["rope"].h, w1)
                        for h in range(2):
                            s.dma_start(out=qscr[h, bt, 0:32, pos0:pos0 + GRP],
                                        in_=qqstage[p][h * 32:(h + 1) * 32, 0:GRP]
                                        ).then_inc(SM["spill"].h, 16)
                            s.dma_start(out=qscr[h, bt, 32:64, pos0:pos0 + GRP],
                                        in_=qqstage[p][h * 32:(h + 1) * 32, GRP:2 * GRP]
                                        ).then_inc(SM["spill"].h, 16)
                    p1.on("gpsimd", sy_spq)

                p1.emit(nc)

            with ExitStack() as rb2:
                E3 = rb2.enter_context
                OT2 = [E3(nc.sbuf_tensor(f"OT2_{i}", [64, 2 * S], BF16)) for i in range(2)]
                kbuf = [E3(nc.sbuf_tensor(f"kbuf{i}", [128, S], BF16)) for i in range(2)]
                qbuf = [E3(nc.sbuf_tensor(f"qbuf{i}", [128, S], BF16)) for i in range(2)]
                pbuf = [E3(nc.sbuf_tensor(f"pbuf{i}", [128, 2 * QT], BF16)) for i in range(2)]
                lden2 = [E3(nc.sbuf_tensor(f"lden{i}", [NQT, QT], F32))
                         for i in range(2)]
                lseg = E3(nc.sbuf_tensor("lseg", [1, 2 * QT], F32))
                linvh2 = [E3(nc.sbuf_tensor(f"linvh{i}", [NQT, QT], F32))
                          for i in range(2)]
                spair = [E3(nc.psum_tensor(f"spair{i}", [128, 2 * QT], F32)) for i in range(2)]
                obank = [E3(nc.psum_tensor(f"obank{i}", [65, QT], F32)) for i in range(2)]
                lps2 = [E3(nc.psum_tensor(f"lps2{i}", [64, QT], F32)) for i in range(2)]

                p2 = Prog()
                KTS_HB = 2 * NQT * (NQT + 1)

                def mk_ld(hb):
                    h, b = hb // 2, hb % 2
                    w_smm = (KTS_HB // 2) * (hb - 1) if hb >= 2 else 0
                    SM["kq"].tick(32)

                    def ld(s, hb=hb, h=h, b=b, w=w_smm):
                        if hb >= 2:
                            s.wait_ge(SM["smm"].h, w)
                        s.dma_start(out=kbuf[hb % 2][0:64, :], in_=kscr[h, b]
                                    ).then_inc(SM["kq"].h, 16)
                        s.dma_start(out=qbuf[hb % 2][0:64, :], in_=qscr[h, b]
                                    ).then_inc(SM["kq"].h, 16)
                    return ld

                p2.on("sync", mk_ld(0))
                p2.on("sync", mk_ld(1))

                SM["kz"].tick(4)

                def kz_v(v):
                    v.memset(kbuf[0][64:128, :], 0.0).then_inc(SM["kz"].h, 1)
                    v.memset(kbuf[1][64:128, :], 0.0).then_inc(SM["kz"].h, 1)
                p2.on("vector", kz_v)

                def kz_g(g):
                    g.memset(qbuf[0][64:128, :], 0.0).then_inc(SM["kz"].h, 1)
                    g.memset(qbuf[1][64:128, :], 0.0).then_inc(SM["kz"].h, 1)
                p2.on("gpsimd", kz_g)

                pend_items = []
                pend_rh = None
                for hb in range(4):
                    h, b = hb // 2, hb % 2
                    if hb + 2 < 4:
                        p2.on("sync", mk_ld(hb + 2))
                    new_items = []
                    for qt in range(NQT):
                        iq = hb * NQT + qt
                        ktn = 4 * (qt + 1)
                        qtp = qt % 2
                        pvq = []
                        for pj in range(ktn // 2):
                            jj = SM["smm"].n
                            kb, qb = kbuf[hb % 2], qbuf[hb % 2]
                            sb, pb = spair[jj % 2], pbuf[jj % 2]
                            smm_t = SM["smm"].tick()

                            def te_s(te, qt=qt, pj=pj, jj=jj, kb=kb, qb=qb, sb=sb,
                                     w_kq=32 * (hb + 1), hb=hb,
                                     first=(qt == 0 and pj == 0)):
                                if first:
                                    te.wait_ge(SM["kq"].h, w_kq)
                                    if hb == 0:
                                        te.wait_ge(SM["kz"].h, 4)
                                        for i in range(12):
                                            te.matmul(
                                                sb[:, (i % 2) * QT:(i % 2) * QT + QT],
                                                masks_sb[:, (i % 4) * 128:
                                                         (i % 4) * 128 + 128],
                                                masks_sb[:, 0:QT],
                                                start=True, stop=True,
                                                skip_group_check=True)
                                if jj >= 2:
                                    te.wait_ge(SM["exp"].h, jj - 1)
                                for u in range(2):
                                    kt = 2 * pj + u
                                    ins = te.matmul(
                                        sb[:, u * QT:(u + 1) * QT],
                                        kb[:, kt * 128:(kt + 1) * 128],
                                        qb[:, qt * QT:(qt + 1) * QT],
                                        start=True, stop=True,
                                        skip_group_check=True)
                                ins.then_inc(SM["smm"].h, 1)
                            p2.on("tensor", te_s)

                            exp_t = SM["exp"].tick()

                            def sc_e(s, jj=jj, sb=sb, pb=pb, w=smm_t):
                                s.wait_ge(SM["smm"].h, w)
                                if jj >= 2:
                                    s.wait_ge(SM["pv"].h, jj - 1)
                                s.activation(pb[:, :], sb[:, :], AF.Exp,
                                             bias=ebias[:, :], scale=0.125
                                             ).then_inc(SM["exp"].h, 1)
                            p2.on("scalar", sc_e)

                            diag = [(u, 2 * pj + u - 4 * qt) for u in range(2)
                                    if 2 * pj + u >= 4 * qt]
                            if diag:
                                SM["msk"].tick(len(diag))

                                def ve_m(v, pb=pb, diag=diag, w=exp_t):
                                    v.wait_ge(SM["exp"].h, w)
                                    for (u, dx) in diag:
                                        v.tensor_mul(pb[:, u * QT:(u + 1) * QT],
                                                     pb[:, u * QT:(u + 1) * QT],
                                                     masks_sb[:, dx * QT:(dx + 1) * QT]
                                                     ).then_inc(SM["msk"].h, 1)
                                p2.on("vector", ve_m)
                            msk_t = SM["msk"].n
                            pv_t = SM["pv"].tick()

                            def te_pv(te, hb=hb, qt=qt, pj=pj, iq=iq, pb=pb,
                                      qtp=qtp, ktn=ktn, w_e=exp_t, w_m=msk_t,
                                      dg=bool(diag), first=(pj == 0)):
                                if first and iq >= 2:
                                    te.wait_ge(SM["ot"].h, iq - 1)
                                    te.wait_ge(SM["lsh"].h, iq - 1)
                                te.wait_ge(SM["exp"].h, w_e)
                                if dg:
                                    te.wait_ge(SM["msk"].h, w_m)
                                for u in range(2):
                                    kt = 2 * pj + u
                                    base = hb * NKT * 65 + kt * 65
                                    ins = te.matmul(
                                        obank[qtp][:, :], V4[:, base:base + 65],
                                        pb[:, u * QT:(u + 1) * QT],
                                        start=(kt == 0), stop=(kt == ktn - 1),
                                        skip_group_check=True)
                                ins.then_inc(SM["pv"].h, 1)
                            pvq.append(te_pv)
                            if len(pvq) > 2:
                                p2.on("tensor", pvq.pop(0))
                        for fn in pvq:
                            p2.on("tensor", fn)

                        SM["lsh"].tick(1)

                        SM["ot"].tick()

                        def ve_c(v, h=h, b=b, qt=qt, qtp=qtp, iq=iq,
                                 w=SM["pv"].n):
                            if iq >= 2:
                                v.wait_ge(SM["lmv"].h, 16 * (iq - 1))
                            v.wait_ge(SM["pv"].h, w)
                            v.tensor_copy(
                                OT2[h][:, b * S + qt * QT:b * S + (qt + 1) * QT],
                                obank[qtp][0:64, :]).then_inc(SM["ot"].h, 1)
                            v.tensor_copy(lseg[0:1, (qt % 2) * QT:
                                               (qt % 2) * QT + QT],
                                          obank[qtp][64:65, :]
                                          ).then_inc(SM["lsh"].h, 1)
                        p2.on("vector", ve_c)

                        SM["lmv"].tick(16)

                        def g_mv(g, qt=qt, hb=hb, iq=iq, first=(qt == 0)):
                            if first and hb >= 2:
                                g.wait_ge(SM["lr"].h, hb - 1)
                            g.wait_ge(SM["lsh"].h, iq + 1)
                            g.dma_start(out=lden2[hb % 2][qt:qt + 1, :],
                                        in_=lseg[0:1, (qt % 2) * QT:
                                                 (qt % 2) * QT + QT]
                                        ).then_inc(SM["lmv"].h, 16)
                        p2.on("gpsimd", g_mv)

                        gq = iq
                        lrp_t = SM["lrp"].tick()

                        def te_rp2(te, qt=qt, hb=hb, gq=gq):
                            te.wait_ge(SM["lr"].h, hb + 1)
                            if gq >= 2:
                                te.wait_ge(SM["lcp"].h, gq - 1)
                            te.matmul(lps2[qt % 2][:, :],
                                      oh16[0:NQT, qt * 64:(qt + 1) * 64],
                                      linvh2[hb % 2][0:NQT, :],
                                      start=True, stop=True,
                                      skip_group_check=True
                                      ).then_inc(SM["lrp"].h, 1)

                        SM["lcp"].tick()

                        def ve_f(v, h=h, b=b, qt=qt, w=lrp_t):
                            v.wait_ge(SM["lrp"].h, w)
                            v.tensor_mul(
                                OT2[h][:, b * S + qt * QT:b * S + (qt + 1) * QT],
                                OT2[h][:, b * S + qt * QT:b * S + (qt + 1) * QT],
                                lps2[qt % 2][:, :]).then_inc(SM["lcp"].h, 1)

                        new_items.append((te_rp2, ve_f))

                        if pend_rh is not None and qt == NQT // 2:
                            p2.on("vector", pend_rh)
                            pend_rh = None
                        if pend_items and qt > NQT // 2:
                            k = -(-len(pend_items) // (NQT - qt))
                            for _ in range(k):
                                tfn, vfn = pend_items.pop(0)
                                p2.on("tensor", tfn)
                                p2.on("vector", vfn)

                    SM["lr"].tick()

                    def ve_rh(v, hb=hb, w=16 * NQT * (hb + 1)):
                        if hb >= 2:
                            v.wait_ge(SM["lrp"].h, NQT * (hb - 1))
                        v.wait_ge(SM["lmv"].h, w)
                        v.reciprocal(linvh2[hb % 2][:, :], lden2[hb % 2][:, :]
                                     ).then_inc(SM["lr"].h, 1)

                    assert not pend_items
                    pend_items = new_items
                    pend_rh = ve_rh

                if pend_rh is not None:
                    p2.on("vector", pend_rh)
                for tfn, vfn in pend_items:
                    p2.on("tensor", tfn)
                    p2.on("vector", vfn)

                p2.emit(nc)

                p3 = Prog()

                def b3s(s):
                    ogr = og_l.rearrange("(c p) t -> p c t", p=128)
                    s.dma_start(out=ogr[0:64],
                                in_=OT2[0][:, :].rearrange("p (c t) -> p c t", c=n_cores)
                                ).then_inc(SM["og"].h, 16)
                    s.dma_start(out=ogr[64:128],
                                in_=OT2[1][:, :].rearrange("p (c t) -> p c t", c=n_cores)
                                ).then_inc(SM["og"].h, 16)
                p3.on("sync", b3s)

                def b3g(g):
                    g.wait_ge(SM["og"].h, 32)
                    with allow_ici_all_to_all():
                        g.collective_compute(
                            "AllToAll", ALU.bypass,
                            replica_groups=[list(range(n_cores))],
                            ins=[og_l.opt()], outs=[og_m.opt()],
                        ).then_inc(SM["cc"].h, 1)
                    g.wait_ge(SM["cc"].h, 1)
                p3.on("gpsimd", b3g)
                p3.emit(nc)

        xnq = E(nc.sbuf_tensor("xnq", [128, 8 * NT], BF16))
        accum = E(nc.sbuf_tensor("accum", [128, 8 * NT], F32))
        gsb = E(nc.sbuf_tensor("gsb", [8, NT], F32))

        with ExitStack() as rb4:
            E4 = rb4.enter_context
            otf = E4(nc.sbuf_tensor("otf", [128, 8 * NT], BF16))
            wosb = E4(nc.sbuf_tensor("wosb", [128, 8 * D], BF16))
            rtsb = E4(nc.sbuf_tensor("rtsb", [128, 8 * 8], BF16))
            xm = [E4(nc.sbuf_tensor("xm0", [128, 4 * D], F32))]
            xn4 = [E4(nc.sbuf_tensor(f"xn4{i}", [128, D], BF16)) for i in range(4)]
            xsq4 = [E4(nc.sbuf_tensor(f"xsq4{i}", [128, D], F32)) for i in range(4)]
            st4 = E4(nc.sbuf_tensor("st4", [128, 64], F32))
            gtmp = E4(nc.sbuf_tensor("gtmp", [128, 96], F32))
            wops = [E4(nc.psum_tensor(f"wops{i}", [128, QT], F32)) for i in range(2)]
            xps4 = [E4(nc.psum_tensor(f"xps4{i}", [128, 8 * 128], BF16)) for i in range(3)]
            lgps = [E4(nc.psum_tensor(f"lgps{i}", [128, 8], F32)) for i in range(2)]
            gtps = E4(nc.psum_tensor("gtps", [128, 128], F32))

            p4 = Prog()

            def b4_l(s):
                s.dma_start(out=otf[:, :].rearrange("p (c t) -> p c t", c=8),
                            in_=og_m.rearrange("(c p) t -> p c t", p=128)
                            ).then_inc(SM["otf"].h, 16)
            p4.on("sync", b4_l)

            def b4_g(g):
                g.dma_start(out=wosb[:, :].rearrange("p (c o) -> p c o", c=8),
                            in_=wo_t.rearrange("(c p) o -> p c o", p=128)
                            ).then_inc(SM["wocast"].h, 16)
                g.dma_start(out=rtsb[:, :].rearrange("p (c e) -> p c e", c=8),
                            in_=rt_in.rearrange("(c p) e -> p c e", p=128)
                            ).then_inc(SM["rtc"].h, 16)
            p4.on("gpsimd", b4_g)

            for ob in range(8):
                for q in range(NTQ):
                    iw = ob * NTQ + q
                    womm_t = SM["womm"].tick()

                    def te_wo(te, ob=ob, q=q, iw=iw):
                        if iw == 0:
                            te.wait_ge(SM["otf"].h, 16)
                            te.wait_ge(SM["wocast"].h, 16)
                        if iw >= 2:
                            te.wait_ge(SM["woev"].h, iw - 1)
                        for k in range(8):
                            ins = te.matmul(
                                wops[iw % 2][:, :],
                                wosb[:, k * D + ob * 128:k * D + (ob + 1) * 128],
                                otf[:, k * NT + q * QT:k * NT + (q + 1) * QT],
                                start=(k == 0), stop=(k == 7))
                        ins.then_inc(SM["womm"].h, 1)
                    p4.on("tensor", te_wo)

                    SM["woev"].tick()

                    def sc_wo(s, ob=ob, q=q, iw=iw, w=womm_t):
                        s.wait_ge(SM["womm"].h, w)
                        s.copy(accum[:, ob * NT + q * QT:ob * NT + (q + 1) * QT],
                               wops[iw % 2][:, :]).then_inc(SM["woev"].h, 1)
                    p4.on("scalar", sc_wo)

            for gg in range(NX4):
                w_xn = 4 * G + 4 * gg
                x_t = SM["xin"].tick(16)

                def b4x(s, gg=gg, w_xn=w_xn):
                    if gg >= 1:
                        s.wait_ge(SM["xn"].h, w_xn)
                    s.dma_start(
                        out=xm[0][:, :].rearrange("p (t d) -> p t d", t=4),
                        in_=x_mine[gg * GRP:(gg + 1) * GRP, :]
                        .rearrange("(t p) d -> p t d", p=128)
                    ).then_inc(SM["xin"].h, 16)
                p4.on("sync", b4x)

                for t in range(4):
                    emit_ln_sq(p4, SM, xm[0], t, 4 * G + 4 * gg + t, xsq4, x_t,
                               st4)
                for tp in range(2):
                    emit_ln_stats_pair(p4, SM, xm[0], 2 * tp, 2 * tp + 1,
                                       4 * G + 4 * gg + 2 * tp,
                                       4 * G + 4 * gg + 2 * tp + 1,
                                       xsq4, st4, x_t)
                nbts4 = []
                for t in range(4):
                    emit_ln_nb(p4, SM, 4 * G + 4 * gg + t, st4)
                    nbts4.append(SM["vnb"].n)
                for t in range(4):
                    i = 4 * G + 4 * gg + t
                    emit_ln_fin(p4, SM, xm[0], t, i, st4, xn4, nbts4[t])
                for t in range(4):
                    i = 4 * G + 4 * gg + t
                    emit_tr_evac(p4, SM, xn4, xps4, identb, i, xnq,
                                 gg * GRP + t * 128, kvq_guard=None)

            w_xnq = SM["xnTe"].n
            for tbp in range(NTT // 2):
                tbs = (2 * tbp, 2 * tbp + 1)
                lgt_B = None
                for tb in tbs:
                    lgt_B = SM["lgt"].tick()

                    def te_lg(te, tb=tb, w=w_xnq):
                        if tb == 0:
                            te.wait_ge(SM["xnTe"].h, w)
                            te.wait_ge(SM["rtc"].h, 16)
                        if tb >= 2:
                            te.wait_ge(SM["rtv"].h, tb - 1)
                        for k in range(8):
                            ins = te.matmul(lgps[tb % 2][:, :],
                                            xnq[:, k * NT + tb * 128:k * NT + (tb + 1) * 128],
                                            rtsb[:, k * 8:(k + 1) * 8],
                                            start=(k == 0), stop=(k == 7),
                                            skip_group_check=True)
                        ins.then_inc(SM["lgt"].h, 1)
                    p4.on("tensor", te_lg)

                rte_prev = SM["rte"].n
                SM["rtv"].tick()
                rtv_B = SM["rtv"].tick()

                def ve_r2(v, tbs=tbs, w=lgt_B, w_e=rte_prev):
                    v.wait_ge(SM["lgt"].h, w)
                    if w_e >= 1:
                        v.wait_ge(SM["rte"].h, w_e)
                    zz = [lgps[tb % 2][:, :] for tb in tbs]
                    for z, cb in zip(zz, (0, 48)):
                        v.reduce_max(gtmp[:, cb + 0:cb + 1], z, axis=AX.X)
                    for z, cb in zip(zz, (0, 48)):
                        v.tensor_scalar(gtmp[:, cb + 1:cb + 9], z,
                                        gtmp[:, cb + 0:cb + 1], None, ALU.is_ge)
                    for z, cb in zip(zz, (0, 48)):
                        v.scalar_tensor_tensor(gtmp[:, cb + 13:cb + 21],
                                               gtmp[:, cb + 1:cb + 9], -30000.0,
                                               z, ALU.mult, ALU.add)
                    for cb in (0, 48):
                        v.reduce_max(gtmp[:, cb + 9:cb + 10],
                                     gtmp[:, cb + 13:cb + 21], axis=AX.X)
                    for cb in (0, 48):
                        v.tensor_sub(gtmp[:, cb + 45:cb + 46],
                                     gtmp[:, cb + 9:cb + 10],
                                     gtmp[:, cb + 0:cb + 1]
                                     ).then_inc(SM["rtv"].h, 1)
                p4.on("vector", ve_r2)

                SM["rte"].tick()
                rte_B = SM["rte"].tick()

                def sc_r(s, w=rtv_B):
                    s.wait_ge(SM["rtv"].h, w)
                    s.activation(gtmp[:, 10:11], gtmp[:, 45:46], AF.Exp
                                 ).then_inc(SM["rte"].h, 1)
                    s.activation(gtmp[:, 58:59], gtmp[:, 93:94], AF.Exp
                                 ).then_inc(SM["rte"].h, 1)
                p4.on("scalar", sc_r)

                SM["gG"].tick()
                gG_B = SM["gG"].tick()

                def ve_r4(v, w=rte_B):
                    v.wait_ge(SM["rte"].h, w)
                    c = gtmp
                    for cb in (0, 48):
                        v.tensor_scalar(c[:, cb + 46:cb + 47], c[:, cb + 10:cb + 11],
                                        1.0, None, ALU.add)
                    for cb in (0, 48):
                        v.reciprocal(c[:, cb + 11:cb + 12], c[:, cb + 46:cb + 47])
                    for cb in (0, 48):
                        v.tensor_scalar(c[:, cb + 12:cb + 13], c[:, cb + 11:cb + 12],
                                        -1.0, 1.0, ALU.mult, ALU.add)
                    for cb in (0, 48):
                        v.tensor_scalar(c[:, cb + 21:cb + 29], c[:, cb + 13:cb + 21],
                                        c[:, cb + 9:cb + 10], None, ALU.is_ge)
                    for cb in (0, 48):
                        v.tensor_scalar(c[:, cb + 29:cb + 37], c[:, cb + 1:cb + 9],
                                        c[:, cb + 11:cb + 12], None, ALU.mult)
                    for cb in (0, 48):
                        v.scalar_tensor_tensor(c[:, cb + 37:cb + 45],
                                               c[:, cb + 21:cb + 29],
                                               c[:, cb + 12:cb + 13],
                                               c[:, cb + 29:cb + 37],
                                               ALU.mult, ALU.add
                                               ).then_inc(SM["gG"].h, 1)
                p4.on("vector", ve_r4)

                for (tb, cb) in zip(tbs, (0, 48)):
                    gev_prev = SM["gev"].n
                    gtr_t = SM["gtr"].tick()

                    def te_gt(te, cb=cb, w=gG_B, wc=gev_prev):
                        te.wait_ge(SM["gG"].h, w)
                        if wc >= 1:
                            te.wait_ge(SM["gev"].h, wc)
                        te.transpose(gtps[0:8, 0:128], gtmp[:, cb + 37:cb + 45],
                                     identf[:, :]).then_inc(SM["gtr"].h, 1)
                    p4.on("tensor", te_gt)

                    SM["gev"].tick()

                    def sc_gt(s, tb=tb, w=gtr_t):
                        s.wait_ge(SM["gtr"].h, w)
                        s.copy(gsb[0:8, tb * 128:(tb + 1) * 128], gtps[0:8, 0:128]
                               ).then_inc(SM["gev"].h, 1)
                    p4.on("scalar", sc_gt)

            p4.emit(nc)

        if debug:
            pD = Prog()
            b0w = SM["b0"].tick(64)

            def bdg(s, w=b0w):
                s.dma_start(out=dbg_acc, in_=accum[:, :]).then_inc(SM["b0"].h, 16)
                s.dma_start(out=dbg_gsb, in_=gsb[0:8, :]).then_inc(SM["b0"].h, 16)
                s.dma_start(out=dbg_xnq, in_=xnq[:, :]).then_inc(SM["b0"].h, 16)
                s.dma_start(out=dbg_ogm, in_=og_m).then_inc(SM["b0"].h, 16)
                s.wait_ge(SM["b0"].h, w)
            pD.on("sync", bdg)
            pD.emit(nc)

        with ExitStack() as rb5:
            E5 = rb5.enter_context
            hsq = E5(nc.sbuf_tensor("hsq", [128, 16 * NT], BF16))
            gsr2 = E5(nc.sbuf_tensor("gsr2", [1, 2 * NT], F32))
            wfcsb = [E5(nc.sbuf_tensor(f"wfc{i}", [128, 8 * 256], BF16)) for i in range(2)]
            wprsb = [E5(nc.sbuf_tensor(f"wpr{i}", [128, 4 * 128], BF16)) for i in range(2)]
            g128 = [E5(nc.sbuf_tensor(f"g128_{i}", [128, QT], F32)) for i in range(NTQ)]
            rbuf = [E5(nc.sbuf_tensor(f"rbuf{i}", [128, QT], BF16)) for i in range(2)]
            yps = [E5(nc.psum_tensor(f"yps{i}", [128, QT], F32)) for i in range(8)]
            hps = [yps[0], yps[1]]
            gqp = yps[2]

            wfc_guard = [0, 0]
            wpr_guard = [0, 0]

            p5 = Prog()
            for e in range(NEXP):
                gated = e < 8
                yev_prev = SM["yev"].n
                sqd_prev_e = SM["sqd"].n

                if gated:
                    grp_prev = NTQ * (e - 1) if e >= 2 else 0
                    gext_t = SM["grep"].tick(16)

                    def sc_gx(s, e=e, w=grp_prev):
                        if w > 0:
                            s.wait_ge(SM["grp"].h, w)
                        s.dma_start(out=gsr2[0:1, (e % 2) * NT:(e % 2 + 1) * NT],
                                    in_=gsb[e:e + 1, :]).then_inc(SM["grep"].h, 16)
                    p5.on("scalar", sc_gx)

                    for q in range(NTQ):
                        gcp_prev = SM["gcp"].n
                        grp_t = SM["grp"].tick()

                        def te_gq(te, e=e, q=q, w_yev=yev_prev, wc=gcp_prev,
                                  w_g=gext_t):
                            if q == 0:
                                te.wait_ge(SM["yev"].h, w_yev)
                                te.wait_ge(SM["grep"].h, w_g)
                            if wc >= 1:
                                te.wait_ge(SM["gcp"].h, wc)
                            te.matmul(gqp[:, :], ones_f[0:1, 0:128],
                                      gsr2[0:1, (e % 2) * NT + q * QT:
                                           (e % 2) * NT + (q + 1) * QT],
                                      start=True, stop=True, skip_group_check=True
                                      ).then_inc(SM["grp"].h, 1)
                        p5.on("tensor", te_gq)

                        gcp_t = SM["gcp"].tick()

                        def sc_gq(s, q=q, w=grp_t, wx=sqd_prev_e):
                            s.wait_ge(SM["grp"].h, w)
                            if wx >= 1:
                                s.wait_ge(SM["sqd"].h, wx)
                            s.copy(g128[q][:, :], gqp[:, :]).then_inc(SM["gcp"].h, 1)
                        p5.on("scalar", sc_gq)
                gcp_all = SM["gcp"].n

                for dh in range(2):
                    ph2_prev_ymm = SM["ymm"].n
                    pend = [None]
                    for u2 in range(16 * NTQ):
                        cp = u2 // (2 * NTQ)
                        u = (u2 // NTQ) % 2
                        q = u2 % NTQ
                        cl = cp * 2 + u
                        if u == 0 and q == 0:
                            w_buf = wfc_guard[cp % 2]
                            wfcd_t = SM["wfcd"].tick(16)

                            def gp_fc(g, e=e, dh=dh, cp=cp, w_h=w_buf):
                                if w_h >= 1:
                                    g.wait_ge(SM["hmm"].h, w_h)
                                c0 = (dh * 16 + cp * 2) * 128
                                g.dma_start(
                                    out=wfcsb[cp % 2][:, :].rearrange(
                                        "p (k f) -> p k f", k=8),
                                    in_=wfc_t[e][:, c0:c0 + 256]
                                    .rearrange("(k p) f -> p k f", p=128)
                                ).then_inc(SM["wfcd"].h, 16)
                            p5.on("gpsimd", gp_fc)

                        hc = SM["hmm"].n
                        hmm_t = SM["hmm"].tick()

                        def te_h(te, q=q, u=u, hc=hc, cp=cp, w_d=wfcd_t,
                                 e=e, gated=gated,
                                 first=(dh == 0 and cp == 0 and u == 0 and q == 0),
                                 w_yv=yev_prev):
                            te.wait_ge(SM["wfcd"].h, w_d)
                            if (not gated) and first:
                                te.wait_ge(SM["yev"].h, w_yv)
                            if hc >= 2:
                                te.wait_ge(SM["relu"].h, hc - 1)
                            for k in range(8):
                                ins = te.matmul(
                                    hps[hc % 2][:, :],
                                    wfcsb[cp % 2][:, k * 256 + u * 128:
                                                  k * 256 + (u + 1) * 128],
                                    xnq[:, k * NT + q * QT:
                                        k * NT + (q + 1) * QT],
                                    start=(k == 0), stop=(k == 7),
                                    skip_group_check=True)
                            ins.then_inc(SM["hmm"].h, 1)
                        p5.on("tensor", te_h)

                        sqv_free = SM["sqv"].n - 1
                        relu_t = SM["relu"].tick()

                        def sc_h(s, hc=hc, w=hmm_t, w_s=sqv_free):
                            s.wait_ge(SM["hmm"].h, w)
                            if hc >= 2:
                                s.wait_ge(SM["sqv"].h, w_s)
                            s.activation(rbuf[hc % 2][:, :], hps[hc % 2][:, :],
                                         AF.Relu).then_inc(SM["relu"].h, 1)
                        p5.on("scalar", sc_h)

                        prev = pend[0]
                        pend[0] = (cl, q)
                        SM["sqv"].tick()
                        if gated and prev is not None:
                            SM["sqd"].tick()

                        def ve_h(v, cl=cl, q=q, hc=hc, w=relu_t,
                                 w_y=ph2_prev_ymm, gated=gated, w_g=gcp_all,
                                 prev=prev):
                            v.wait_ge(SM["relu"].h, w)
                            if w_y > 0 and cl == 0 and q == 0:
                                v.wait_ge(SM["ymm"].h, w_y)
                            if gated and cl == 0 and q == 0 and w_g >= 1:
                                v.wait_ge(SM["gcp"].h, w_g)
                            dst = hsq[:, cl * NT + q * QT:cl * NT + (q + 1) * QT]
                            v.tensor_mul(dst, rbuf[hc % 2][:, :],
                                         rbuf[hc % 2][:, :]).then_inc(SM["sqv"].h, 1)
                            if gated and prev is not None:
                                pcl, pq = prev
                                pdst = hsq[:, pcl * NT + pq * QT:
                                           pcl * NT + (pq + 1) * QT]
                                v.tensor_mul(pdst, pdst, g128[pq][:, :]
                                             ).then_inc(SM["sqd"].h, 1)
                        p5.on("vector", ve_h)
                        if u == 1 and q == NTQ - 1:
                            wfc_guard[cp % 2] = SM["hmm"].n

                    if gated:
                        SM["sqd"].tick()

                        def ve_fl(v, prev=pend[0]):
                            pcl, pq = prev
                            pdst = hsq[:, pcl * NT + pq * QT:pcl * NT + (pq + 1) * QT]
                            v.engine_nop()
                            v.tensor_mul(pdst, pdst, g128[pq][:, :]
                                         ).then_inc(SM["sqd"].h, 1)
                        p5.on("vector", ve_fl)

                    sq_half_done = SM["sqd"].n if gated else SM["sqv"].n
                    for ob in range(8):
                        dbg = dh * 8 + ob
                        bs = (dbg % 2) * 4
                        yev_need = SM["yev"].n - 1
                        for cq in range(4):
                            w_buf = wpr_guard[cq % 2]
                            wprd_t = SM["wprd"].tick(16)

                            def gp_pr(g, e=e, dh=dh, ob=ob, cq=cq, w_y=w_buf):
                                if w_y >= 1:
                                    g.wait_ge(SM["ymm"].h, w_y)
                                r0 = dh * 2048 + cq * 512
                                g.dma_start(
                                    out=wprsb[cq % 2][:, :].rearrange(
                                        "p (u o) -> p u o", u=4),
                                    in_=wpr_t[e][r0:r0 + 512, ob * 128:(ob + 1) * 128]
                                    .rearrange("(u p) o -> p u o", p=128)
                                ).then_inc(SM["wprd"].h, 16)
                            p5.on("gpsimd", gp_pr)

                            for u in range(4):
                                cc = cq * 4 + u
                                ymm_t = SM["ymm"].tick()

                                def te_y(te, cc=cc, cq=cq, u=u, bs=bs, w_d=wprd_t,
                                         w_s=sq_half_done, w_e=yev_need,
                                         sq_sem=("sqd" if gated else "sqv"),
                                         first=(cc == 0)):
                                    te.wait_ge(SM["wprd"].h, w_d)
                                    if first:
                                        te.wait_ge(SM[sq_sem].h, w_s)
                                        if w_e >= 1:
                                            te.wait_ge(SM["yev"].h, w_e)
                                    for q in range(NTQ):
                                        ins = te.matmul(
                                            yps[bs + q][:, :],
                                            wprsb[cq % 2][:, u * 128:(u + 1) * 128],
                                            hsq[:, cc * NT + q * QT:cc * NT + (q + 1) * QT],
                                            start=(cc == 0), stop=(cc == 15),
                                            skip_group_check=True)
                                    ins.then_inc(SM["ymm"].h, 1)
                                p5.on("tensor", te_y)
                            wpr_guard[cq % 2] = SM["ymm"].n

                        ymm_all = SM["ymm"].n
                        SM["yev"].tick()

                        def ve_y(v, ob=ob, bs=bs, w=ymm_all):
                            v.wait_ge(SM["ymm"].h, w)
                            for q in range(NTQ):
                                ins = v.scalar_tensor_tensor(
                                    accum[:, ob * NT + q * QT:ob * NT + (q + 1) * QT],
                                    yps[bs + q][:, :], 1.0,
                                    accum[:, ob * NT + q * QT:ob * NT + (q + 1) * QT],
                                    ALU.mult, ALU.add)
                            ins.then_inc(SM["yev"].h, 1)
                        p5.on("vector", ve_y)

            p5.emit(nc)

        with ExitStack() as rb6:
            E6 = rb6.enter_context
            xr = [E6(nc.sbuf_tensor(f"xr{i}", [128, 4 * D], F32)) for i in range(2)]
            osb = [E6(nc.sbuf_tensor(f"osb{i}", [128, D], F32)) for i in range(2)]
            eps = [E6(nc.psum_tensor(f"eps{i}", [128, 1024], F32)) for i in range(2)]

            p6 = Prog()
            for tt in range(NTT):
                if tt % 4 == 0:
                    gg = tt // 4
                    epv_w = 4 * gg - 4
                    SM["epx"].tick(16)

                    def b6x(s, gg=gg, w=epv_w):
                        if gg >= 2:
                            s.wait_ge(SM["epv"].h, w)
                        s.dma_start(
                            out=xr[gg % 2][:, :].rearrange("p (t d) -> p t d", t=4),
                            in_=x_mine[gg * GRP:(gg + 1) * GRP, :]
                            .rearrange("(t p) d -> p t d", p=128)
                        ).then_inc(SM["epx"].h, 16)
                    p6.on("sync", b6x)

                ept_t = SM["ept"].tick()

                def te_ep(te, tt=tt):
                    if tt >= 2:
                        te.wait_ge(SM["epv"].h, tt - 1)
                    for k in range(8):
                        ins = te.transpose(eps[tt % 2][:, k * 128:(k + 1) * 128],
                                           accum[:, k * NT + tt * 128:
                                                 k * NT + (tt + 1) * 128],
                                           identf[:, :])
                    ins.then_inc(SM["ept"].h, 1)
                p6.on("tensor", te_ep)

                epx_w = 16 * (tt // 4 + 1)
                out_w = 16 * (tt - 1)
                SM["epv"].tick()

                def ve_ep(v, tt=tt, w_t=ept_t, w_x=epx_w, w_o=out_w):
                    v.wait_ge(SM["ept"].h, w_t)
                    v.wait_ge(SM["epx"].h, w_x)
                    if tt >= 2:
                        v.wait_ge(SM["out"].h, w_o)
                    v.tensor_add(osb[tt % 2][:, :],
                                 xr[(tt // 4) % 2][:, (tt % 4) * D:(tt % 4 + 1) * D],
                                 eps[tt % 2][:, :]).then_inc(SM["epv"].h, 1)
                p6.on("vector", ve_ep)

                epv_t = SM["epv"].n
                SM["out"].tick(16)

                def b6o(s, tt=tt, w=epv_t):
                    s.wait_ge(SM["epv"].h, w)
                    s.dma_start(out=out_mine[tt * 128:(tt + 1) * 128, :],
                                in_=osb[tt % 2][:, :]).then_inc(SM["out"].h, 16)
                p6.on("sync", b6o)

            out_all = SM["out"].n

            def b6f(s, w=out_all):
                s.wait_ge(SM["out"].h, w)
            p6.on("sync", b6f)
            p6.emit(nc)

    return nc


def emit_ln_sq(prog, SM, xsrc, t, i, xsq, xin_target, stats):
    sl = (i % 8) * 8
    def sc_sq(s, t=t, i=i, sl=sl, w_x=xin_target):
        if t == 0:
            s.wait_ge(SM["xin"].h, w_x)
        s.activation(xsq[i % 4][:, :], xsrc[:, t * D:(t + 1) * D], AF.Square,
                     accum_out=stats[:, sl + 1:sl + 2]
                     ).then_inc(SM["sq"].h, 1)
    prog.on("scalar", sc_sq)
    SM["sq"].tick()


def emit_ln_stats_pair(prog, SM, xsrc, tA, tB, iA, iB, xsq, stats, xin_target):
    slA = (iA % 8) * 8
    slB = (iB % 8) * 8
    vstat_tB = SM["vstat"].tick() + 1
    SM["vstat"].tick()

    def ve_st(v, tA=tA, tB=tB, iA=iA, iB=iB, slA=slA, slB=slB, w_x=xin_target):
        xtA = xsrc[:, tA * D:(tA + 1) * D]
        xtB = xsrc[:, tB * D:(tB + 1) * D]
        if tA == 0:
            v.wait_ge(SM["xin"].h, w_x)
        v.reduce_sum(stats[:, slA:slA + 1], xtA, axis=AX.X)
        v.reduce_sum(stats[:, slB:slB + 1], xtB, axis=AX.X)
        v.wait_ge(SM["sq"].h, iB + 1)
        for sl in (slA, slB):
            v.tensor_scalar(stats[:, sl + 2:sl + 3], stats[:, sl:sl + 1],
                            1.0 / D, None, ALU.mult)
        for sl in (slA, slB):
            v.tensor_scalar(stats[:, sl + 3:sl + 4], stats[:, sl + 1:sl + 2],
                            1.0 / D, 1e-5, ALU.mult, ALU.add)
        for sl in (slA, slB):
            m = stats[:, sl + 2:sl + 3]
            v.tensor_mul(stats[:, sl + 4:sl + 5], m, m)
        for sl in (slA, slB):
            ins = v.scalar_tensor_tensor(stats[:, sl + 5:sl + 6],
                                         stats[:, sl + 3:sl + 4], 1.0,
                                         stats[:, sl + 4:sl + 5],
                                         ALU.mult, ALU.subtract)
            ins.then_inc(SM["vstat"].h, 1)
    prog.on("vector", ve_st)

    SM["sln"].tick()
    SM["sln"].tick()

    def sc_ln(s, slA=slA, slB=slB, w=vstat_tB):
        s.wait_ge(SM["vstat"].h, w)
        s.activation(stats[:, slA + 6:slA + 7], stats[:, slA + 5:slA + 6], AF.Ln)
        s.activation(stats[:, slB + 6:slB + 7], stats[:, slB + 5:slB + 6], AF.Ln)
        s.activation(stats[:, slA + 6:slA + 7], stats[:, slA + 6:slA + 7],
                     AF.Exp, scale=-0.5).then_inc(SM["sln"].h, 1)
        s.activation(stats[:, slB + 6:slB + 7], stats[:, slB + 6:slB + 7],
                     AF.Exp, scale=-0.5).then_inc(SM["sln"].h, 1)
    prog.on("scalar", sc_ln)


def emit_ln_nb(prog, SM, i, stats):
    sl = (i % 8) * 8
    sln_t = i + 1
    SM["vnb"].tick()

    def ve_nb(v, sl=sl, w=sln_t):
        v.wait_ge(SM["sln"].h, w)
        v.scalar_tensor_tensor(stats[:, sl + 7:sl + 8], stats[:, sl + 2:sl + 3],
                               -1.0, stats[:, sl + 6:sl + 7], ALU.mult, ALU.mult
                               ).then_inc(SM["vnb"].h, 1)
    prog.on("vector", ve_nb)


def emit_ln_fin(prog, SM, xsrc, t, i, stats, xn_out, vnb_t):
    sl = (i % 8) * 8
    ttr_prev = SM["ttr"].n
    SM["xn"].tick()

    def sc_xn(s, t=t, i=i, sl=sl, w=vnb_t, w_t=ttr_prev):
        s.wait_ge(SM["vnb"].h, w)
        if i >= 4:
            s.wait_ge(SM["ttr"].h, w_t - 3)
        s.activation(xn_out[i % 4][:, :], xsrc[:, t * D:(t + 1) * D],
                     AF.Identity, bias=stats[:, sl + 7:sl + 8],
                     scale=stats[:, sl + 6:sl + 7]).then_inc(SM["xn"].h, 1)
    prog.on("scalar", sc_xn)


def emit_tr_evac(prog, SM, xn_bufs, xps_bufs, identb, i, dest, dest_col, kvq_guard):
    xn_t = SM["xn"].n
    xnTe_prev = SM["xnTe"].n
    ttr_t = SM["ttr"].tick()

    def te_tr(te, i=i, w=xn_t, w_e=xnTe_prev):
        te.wait_ge(SM["xn"].h, w)
        if i >= 3:
            te.wait_ge(SM["xnTe"].h, w_e - 2)
        for k in range(8):
            ins = te.transpose(xps_bufs[i % 3][:, k * 128:(k + 1) * 128],
                               xn_bufs[i % 4][:, k * 128:(k + 1) * 128],
                               identb[:, :])
        ins.then_inc(SM["ttr"].h, 1)
    prog.on("tensor", te_tr)

    SM["xnTe"].tick()

    def ve_ev(v, i=i, w=ttr_t, kvq_guard=kvq_guard, dest_col=dest_col):
        v.wait_ge(SM["ttr"].h, w)
        if kvq_guard is not None and kvq_guard > 0:
            v.wait_ge(SM["kvq"].h, kvq_guard)
        v.tensor_copy(dest[:, :].rearrange("p (k s) -> p k s", k=8)
                      [:, :, dest_col:dest_col + 128],
                      xps_bufs[i % 3][:, :].rearrange("p (k s) -> p k s", k=8)
                      ).then_inc(SM["xnTe"].h, 1)
    prog.on("vector", ve_ev)


def emit_rope(v, ps1, ps2, ct, stl, stage, rpt, rope_h):
    v.tensor_mul(rpt[0][:, :], ps1[:, :], ct[:, :])
    v.tensor_mul(rpt[1][:, :], ps2[:, :], stl[:, :])
    v.tensor_mul(rpt[2][:, :], ps1[:, :], stl[:, :])
    v.tensor_mul(rpt[3][:, :], ps2[:, :], ct[:, :])
    v.tensor_sub(stage[:, 0:GRP], rpt[0][:, :], rpt[1][:, :])
    v.tensor_add(stage[:, GRP:2 * GRP], rpt[2][:, :], rpt[3][:, :]
                 ).then_inc(rope_h, 1)


def _host_prep(inputs, seq=8192, n_cores=8):
    S = seq
    NT = 2 * S // n_cores
    X = np.ascontiguousarray(np.asarray(inputs["L0_attn_x"], dtype=np.float32))
    Wq = np.asarray(inputs["L0_attn_Wq"], dtype=np.float32)
    Wk = np.asarray(inputs["L0_attn_Wk"], dtype=np.float32)
    Wv = np.asarray(inputs["L0_attn_Wv"], dtype=np.float32)
    Wo = np.asarray(inputs["L0_attn_Wo"], dtype=np.float32)
    Rw = np.asarray(inputs["L0_ffn_router_w"], dtype=np.float32)
    Wfc = np.asarray(inputs["L0_ffn_w_fc"], dtype=np.float32)
    Wpr = np.asarray(inputs["L0_ffn_w_proj"], dtype=np.float32)
    Sfc = np.asarray(inputs["L0_ffn_shared_fc"], dtype=np.float32)
    Spr = np.asarray(inputs["L0_ffn_shared_proj"], dtype=np.float32)

    wfc_t = np.empty((NEXP, D, DFF), dtype=np.float32)
    wpr_t = np.empty((NEXP, DFF, D), dtype=np.float32)
    for e in range(8):
        wfc_t[e] = Wfc[e].T
        wpr_t[e] = Wpr[e].T
    wfc_t[8] = Sfc.T
    wpr_t[8] = Spr.T

    wo_t = np.ascontiguousarray(Wo.T)
    rt = np.ascontiguousarray(Rw.T)

    inv = 10000.0 ** (-np.arange(32, dtype=np.float64) / 32.0)
    pos = np.arange(S, dtype=np.float64)
    ang = pos[None, :] * inv[:, None]
    ang2 = np.concatenate([ang, ang], axis=0)
    cos_t = np.cos(ang2).astype(np.float32)
    sin_t = np.sin(ang2).astype(np.float32)

    masks = np.zeros((4, 128, QT), dtype=np.float32)
    kk = np.arange(128)[:, None]
    qq = np.arange(QT)[None, :]
    for dx in range(4):
        masks[dx] = (qq >= kk + dx * 128).astype(np.float32)

    ident = np.eye(128, dtype=np.float32)
    ones_r = np.ones((1, 128), dtype=np.float32)
    NQT = S // QT
    onehot = np.zeros((NQT, NQT * 64), dtype=np.float32)
    for q in range(NQT):
        onehot[q, q * 64:(q + 1) * 64] = 1.0

    xflat = X.reshape(2 * S, D)
    in_maps = []
    for c in range(n_cores):
        h0, h1 = 2 * c, 2 * c + 1
        rows = (
            [h0 * 64 + 2 * i for i in range(32)]
            + [h1 * 64 + 2 * i for i in range(32)]
            + [h0 * 64 + 2 * i + 1 for i in range(32)]
            + [h1 * 64 + 2 * i + 1 for i in range(32)]
        )
        vrows = list(range(h0 * 64, h0 * 64 + 64)) + list(range(h1 * 64, h1 * 64 + 64))
        in_maps.append({
            "x_full": X,
            "x_mine": np.ascontiguousarray(xflat[c * NT:(c + 1) * NT]),
            "wq_sl": np.ascontiguousarray(Wq[rows, :].T),
            "wk_sl": np.ascontiguousarray(Wk[rows, :].T),
            "wv_sl": np.ascontiguousarray(Wv[vrows, :].T),
            "wo_t": wo_t,
            "rt": rt,
            "wfc_t": wfc_t,
            "wpr_t": wpr_t,
            "cos_t": cos_t,
            "sin_t": sin_t,
            "masks": masks,
            "onehot": onehot,
            "ident": ident,
            "ones_r": ones_r,
        })
    return in_maps


def kernel(**inputs):
    seq = int(np.asarray(inputs["L0_attn_x"]).shape[1])
    n_cores = 8
    NT = 2 * seq // n_cores
    in_maps = _host_prep(inputs, seq=seq, n_cores=n_cores)
    nc = build_program(seq=seq, n_cores=n_cores)
    res = run_bass_kernel_spmd(nc, in_maps, core_ids=list(range(n_cores)))
    out = np.empty((2 * seq, D), dtype=np.float32)
    for c in range(n_cores):
        out[c * NT:(c + 1) * NT] = res.results[c]["out_mine"]
    return out.reshape(2, seq, D)


# version 15
# speedup vs baseline: 1.0501x; 1.0501x over previous
import numpy as np
from contextlib import ExitStack

import concourse.bass as bass
import concourse.mybir as mybir
from concourse.bass_utils import run_bass_kernel_spmd
from concourse.aot_env import allow_ici_all_to_all

F32 = mybir.dt.float32
BF16 = mybir.dt.bfloat16
AF = mybir.ActivationFunctionType
ALU = mybir.AluOpType
AX = mybir.AxisListType

D = 1024
DFF = 4096
NEXP = 9
GRP = 512
QT = 512
EXP_SHIFT = -4.0


class Sem:
    def __init__(self, nc, stack, name):
        self.h = stack.enter_context(nc.semaphore(name))
        self.n = 0

    def tick(self, k=1):
        self.n += k
        return self.n


class Prog:
    ENGS = ("sync", "gpsimd", "tensor", "vector", "scalar")

    def __init__(self):
        self.ev = {e: [] for e in self.ENGS}

    def on(self, eng, fn):
        self.ev[eng].append(fn)

    def emit(self, nc):
        with nc.Block() as block:
            for name in self.ENGS:
                fns = self.ev[name]
                if not fns:
                    continue

                def run(engine, fns=fns):
                    for f in fns:
                        f(engine)

                getattr(block, name)(run)


def build_program(seq=8192, n_cores=8, debug=False):
    S = seq
    NT = 2 * S // n_cores
    G = 2 * S // GRP
    NQT = S // QT
    NKT = S // 128
    NTQ = NT // QT
    NTT = NT // 128
    NX4 = NT // GRP
    PAIRS_HB = NQT * (NQT + 1)

    nc = bass.Bass("TRN2", target_bir_lowering=False)

    x_full = nc.dram_tensor("x_full", [2, S, D], F32, kind="ExternalInput").ap()
    x_mine = nc.dram_tensor("x_mine", [NT, D], F32, kind="ExternalInput").ap()
    wq_sl = nc.dram_tensor("wq_sl", [D, 128], F32, kind="ExternalInput").ap()
    wk_sl = nc.dram_tensor("wk_sl", [D, 128], F32, kind="ExternalInput").ap()
    wv_sl = nc.dram_tensor("wv_sl", [D, 128], F32, kind="ExternalInput").ap()
    wo_t = nc.dram_tensor("wo_t", [D, D], F32, kind="ExternalInput").ap()
    rt_in = nc.dram_tensor("rt", [D, 8], F32, kind="ExternalInput").ap()
    wfc_t = nc.dram_tensor("wfc_t", [NEXP, D, DFF], F32, kind="ExternalInput").ap()
    wpr_t = nc.dram_tensor("wpr_t", [NEXP, DFF, D], F32, kind="ExternalInput").ap()
    cos_t = nc.dram_tensor("cos_t", [64, S], F32, kind="ExternalInput").ap()
    sin_t = nc.dram_tensor("sin_t", [64, S], F32, kind="ExternalInput").ap()
    masks_in = nc.dram_tensor("masks", [4, 128, QT], F32, kind="ExternalInput").ap()
    oh_in = nc.dram_tensor("onehot", [NQT, NQT * 64], F32, kind="ExternalInput").ap()
    ident_in = nc.dram_tensor("ident", [128, 128], F32, kind="ExternalInput").ap()
    ones_in = nc.dram_tensor("ones_r", [1, 128], F32, kind="ExternalInput").ap()
    out_mine = nc.dram_tensor("out_mine", [NT, D], F32, kind="ExternalOutput").ap()

    dkind = "ExternalOutput" if debug else "Internal"
    kscr = nc.dram_tensor("kscr", [2, 2, 64, S], BF16, kind=dkind).ap()
    qscr = nc.dram_tensor("qscr", [2, 2, 64, S], BF16, kind=dkind).ap()
    og_l = nc.dram_tensor("og_l", [n_cores * 128, NT], BF16).ap()
    og_m = nc.dram_tensor("og_m", [n_cores * 128, NT], BF16).ap()
    if debug:
        dbg_acc = nc.dram_tensor("dbg_acc", [128, 8 * NT], F32, kind="ExternalOutput").ap()
        dbg_gsb = nc.dram_tensor("dbg_gsb", [8, NT], F32, kind="ExternalOutput").ap()
        dbg_xnq = nc.dram_tensor("dbg_xnq", [128, 8 * NT], BF16, kind="ExternalOutput").ap()
        dbg_ogm = nc.dram_tensor("dbg_ogm", [n_cores * 128, NT], BF16, kind="ExternalOutput").ap()

    with ExitStack() as st:
        E = st.enter_context

        identb = E(nc.sbuf_tensor("identb", [128, 128], BF16))
        identf = E(nc.sbuf_tensor("identf", [128, 128], F32))
        ones_f = E(nc.sbuf_tensor("ones_f", [1, 128], F32))
        ebias = E(nc.sbuf_tensor("ebias", [128, 1], F32))

        SM = {}
        for name in ["b0", "xin", "cs", "sq", "vsq", "vstat", "sln", "vnb", "xn",
                     "ttr", "xnTe", "kvq", "rope", "spill", "v4c", "kq",
                     "smm", "exp", "msk", "pv", "lr", "lsh", "lrp", "lcp", "lmv", "kz",
                     "ot", "og", "cc", "otf", "wocast", "rtc", "womm",
                     "woev", "lgt", "rtv", "rte", "gG", "gtr", "gev",
                     "grep", "gsq", "grp", "gcp", "xne", "wfcd", "hmm",
                     "relu", "sqd", "sqv", "wprd", "ymm", "yev", "ept", "epx",
                     "epv", "out"]:
            SM[name] = Sem(nc, st, "s_" + name)

        r1 = ExitStack()
        with r1:
            E1 = r1.enter_context
            V4 = E1(nc.sbuf_tensor("V4", [128, 4 * NKT * 65], BF16))
            masks_sb = E1(nc.sbuf_tensor("masks_sb", [128, 4 * QT], BF16))
            wqsb = E1(nc.sbuf_tensor("wqsb", [128, 8 * 128], BF16))
            wksb = E1(nc.sbuf_tensor("wksb", [128, 8 * 128], BF16))
            wvsb = E1(nc.sbuf_tensor("wvsb", [128, 8 * 128], BF16))
            oh16 = E1(nc.sbuf_tensor("oh16", [NQT, NQT * 64], F32))

            p0 = Prog()

            def b0g(g):
                b0h = SM["b0"].h
                g.dma_start(out=identb[:, :], in_=ident_in).then_inc(b0h, 16)
                g.dma_start(out=identf[:, :], in_=ident_in).then_inc(b0h, 16)
                g.dma_start(out=ones_f[:, :], in_=ones_in).then_inc(b0h, 16)
                g.dma_start(out=masks_sb[:, :].rearrange("p (m s) -> p m s", m=4),
                            in_=masks_in.rearrange("m p s -> p m s")).then_inc(b0h, 16)
                g.dma_start(out=wqsb[:, :].rearrange("p (k m) -> p k m", k=8),
                            in_=wq_sl.rearrange("(k p) m -> p k m", p=128)).then_inc(b0h, 16)
                g.dma_start(out=wksb[:, :].rearrange("p (k m) -> p k m", k=8),
                            in_=wk_sl.rearrange("(k p) m -> p k m", p=128)).then_inc(b0h, 16)
                g.dma_start(out=wvsb[:, :].rearrange("p (k m) -> p k m", k=8),
                            in_=wv_sl.rearrange("(k p) m -> p k m", p=128)).then_inc(b0h, 16)
                g.dma_start(out=oh16[:, :], in_=oh_in).then_inc(b0h, 16)
                g.wait_ge(b0h, 128)

            def b0v(v):
                v.memset(ebias[:, :], EXP_SHIFT)
                for hb in range(4):
                    v.memset(
                        V4[:, hb * NKT * 65:(hb + 1) * NKT * 65]
                        .rearrange("p (k s) -> p k s", k=NKT)[:, :, 64:65], 1.0)

            p0.on("gpsimd", b0g)
            p0.on("vector", b0v)
            p0.emit(nc)

            with ExitStack() as rb1:
                E2 = rb1.enter_context
                xg = [E2(nc.sbuf_tensor(f"xg{i}", [128, 4 * D], F32)) for i in range(4)]
                xnT = [E2(nc.sbuf_tensor(f"xnT{i}", [128, 8 * GRP], BF16)) for i in range(3)]
                xn_sb = [E2(nc.sbuf_tensor(f"xnsb{i}", [128, D], BF16)) for i in range(4)]
                xsq = [E2(nc.sbuf_tensor(f"xsq{i}", [128, D], F32)) for i in range(4)]
                stats = E2(nc.sbuf_tensor("stats", [128, 64], F32))
                kqstage = [E2(nc.sbuf_tensor(f"kqs{i}", [64, 2 * GRP], BF16)) for i in range(3)]
                qqstage = [E2(nc.sbuf_tensor(f"qqs{i}", [64, 2 * GRP], BF16)) for i in range(3)]
                rpt = [E2(nc.sbuf_tensor(f"rpt{i}", [64, GRP], F32)) for i in range(4)]
                ctl = [E2(nc.sbuf_tensor(f"ctl{i}", [64, GRP], F32)) for i in range(4)]
                stl = [E2(nc.sbuf_tensor(f"stl{i}", [64, GRP], F32)) for i in range(4)]
                xps = [E2(nc.psum_tensor(f"xps{i}", [128, 8 * 128], BF16)) for i in range(3)]
                kps1 = E2(nc.psum_tensor("kps1", [64, GRP], F32))
                kps2 = E2(nc.psum_tensor("kps2", [64, GRP], F32))
                qps1 = E2(nc.psum_tensor("qps1", [64, GRP], F32))
                qps2 = E2(nc.psum_tensor("qps2", [64, GRP], F32))
                vps = E2(nc.psum_tensor("vps", [128, GRP], F32))

                p1 = Prog()
                pend_v4 = None
                for g in range(G):
                    bt = g // (G // 2)
                    pos0 = (g % (G // 2)) * GRP
                    p = g % 3
                    p4b = g % 4

                    w_xn = 4 * (g - 3)
                    w_rp = 2 * (g - 3)
                    x_t = SM["xin"].tick(16)
                    c_t = SM["cs"].tick(32)

                    def ld(s, g=g, bt=bt, pos0=pos0, p4b=p4b, w_xn=w_xn):
                        if g >= 4:
                            s.wait_ge(SM["xn"].h, w_xn)
                        s.dma_start(
                            out=xg[p4b][:, :].rearrange("p (t d) -> p t d", t=4),
                            in_=x_full[bt, pos0:pos0 + GRP, :]
                            .rearrange("(t p) d -> p t d", p=128)
                        ).then_inc(SM["xin"].h, 16)
                    p1.on("sync", ld)

                    def ldcs(g_, g=g, bt=bt, pos0=pos0, p4b=p4b, w_rp=w_rp):
                        if g >= 4:
                            g_.wait_ge(SM["rope"].h, w_rp)
                        g_.dma_start(out=ctl[p4b][:, :], in_=cos_t[:, pos0:pos0 + GRP]
                                     ).then_inc(SM["cs"].h, 16)
                        g_.dma_start(out=stl[p4b][:, :], in_=sin_t[:, pos0:pos0 + GRP]
                                     ).then_inc(SM["cs"].h, 16)
                    p1.on("gpsimd", ldcs)

                    for t in range(4):
                        emit_ln_sq(p1, SM, xg[p4b], t, 4 * g + t, xsq, x_t,
                                   stats)
                    if pend_v4 is not None:
                        p1.on("scalar", pend_v4)
                        pend_v4 = None
                    for tp in range(2):
                        emit_ln_stats_pair(p1, SM, xg[p4b], 2 * tp, 2 * tp + 1,
                                           4 * g + 2 * tp, 4 * g + 2 * tp + 1,
                                           xsq, stats, x_t)
                    nbts = []
                    for t in range(4):
                        emit_ln_nb(p1, SM, 4 * g + t, stats)
                        nbts.append(SM["vnb"].n)
                    for t in range(4):
                        i = 4 * g + t
                        emit_ln_fin(p1, SM, xg[p4b], t, i, stats, xn_sb, nbts[t])
                    for t in range(4):
                        i = 4 * g + t
                        emit_tr_evac(p1, SM, xn_sb, xps, identb, i, xnT[p], t * 128,
                                     kvq_guard=(3 * (g - 2) if (t == 0 and g >= 3) else None))

                    w_xnTe = 4 * g + 4
                    kvq_t = SM["kvq"].tick()

                    def te_k(te, p=p, g=g, w1=w_xnTe, w2=2 * g - 1):
                        te.wait_ge(SM["xnTe"].h, w1)
                        if g >= 1:
                            te.wait_ge(SM["rope"].h, w2)
                        for half, dst in ((0, kps1), (1, kps2)):
                            for k in range(8):
                                ins = te.matmul(
                                    dst[:, :],
                                    wksb[:, k * 128 + half * 64:k * 128 + half * 64 + 64],
                                    xnT[p][:, k * GRP:(k + 1) * GRP],
                                    start=(k == 0), stop=(k == 7),
                                    skip_group_check=True)
                        ins.then_inc(SM["kvq"].h, 1)
                    p1.on("tensor", te_k)

                    w_spK = 128 * (g - 3) + 64
                    rope_tK = SM["rope"].tick()

                    def ve_rk(v, p=p, p4b=p4b, g=g, w1=kvq_t, w2=c_t, w3=w_spK):
                        v.wait_ge(SM["kvq"].h, w1)
                        v.wait_ge(SM["cs"].h, w2)
                        if g >= 3:
                            v.wait_ge(SM["spill"].h, w3)
                        emit_rope(v, kps1, kps2, ctl[p4b], stl[p4b], kqstage[p], rpt,
                                  SM["rope"].h)
                    p1.on("vector", ve_rk)

                    SM["spill"].tick(64)

                    def sy_spk(s, p=p, bt=bt, pos0=pos0, w1=rope_tK):
                        s.wait_ge(SM["rope"].h, w1)
                        for h in range(2):
                            s.dma_start(out=kscr[h, bt, 0:32, pos0:pos0 + GRP],
                                        in_=kqstage[p][h * 32:(h + 1) * 32, 0:GRP]
                                        ).then_inc(SM["spill"].h, 16)
                            s.dma_start(out=kscr[h, bt, 32:64, pos0:pos0 + GRP],
                                        in_=kqstage[p][h * 32:(h + 1) * 32, GRP:2 * GRP]
                                        ).then_inc(SM["spill"].h, 16)
                    p1.on("gpsimd", sy_spk)

                    w_v4 = 8 * g
                    kvq_tv = SM["kvq"].tick()

                    def te_v(te, p=p, g=g, w1=w_v4):
                        if g >= 1:
                            te.wait_ge(SM["v4c"].h, w1)
                        for t4 in range(4):
                            for k in range(8):
                                ins = te.matmul(
                                    vps[:, t4 * 128:(t4 + 1) * 128],
                                    xnT[p][:, k * GRP + t4 * 128:k * GRP + (t4 + 1) * 128],
                                    wvsb[:, k * 128:(k + 1) * 128],
                                    start=(k == 0), stop=(k == 7),
                                    skip_group_check=True)
                        ins.then_inc(SM["kvq"].h, 1)
                    p1.on("tensor", te_v)

                    SM["v4c"].tick(8)

                    def sc_v4(s, g=g, bt=bt, pos0=pos0, w1=kvq_tv):
                        s.wait_ge(SM["kvq"].h, w1)
                        for t4 in range(4):
                            kt = pos0 // 128 + t4
                            for h in range(2):
                                base = (h * 2 + bt) * NKT * 65 + kt * 65
                                ins = s.copy(
                                    V4[:, base:base + 64],
                                    vps[:, t4 * 128 + h * 64:t4 * 128 + h * 64 + 64])
                        ins.then_inc(SM["v4c"].h, 8)
                    pend_v4 = sc_v4

                    kvq_tq = SM["kvq"].tick()

                    def te_q(te, p=p, g=g, w1=2 * g):
                        if g >= 1:
                            te.wait_ge(SM["rope"].h, w1)
                        for half, dst in ((0, qps1), (1, qps2)):
                            for k in range(8):
                                ins = te.matmul(
                                    dst[:, :],
                                    wqsb[:, k * 128 + half * 64:k * 128 + half * 64 + 64],
                                    xnT[p][:, k * GRP:(k + 1) * GRP],
                                    start=(k == 0), stop=(k == 7),
                                    skip_group_check=True)
                        ins.then_inc(SM["kvq"].h, 1)
                    p1.on("tensor", te_q)

                    w_spQ = 128 * (g - 2)
                    rope_tQ = SM["rope"].tick()

                    def ve_rq(v, p=p, p4b=p4b, g=g, w1=kvq_tq, w3=w_spQ):
                        v.wait_ge(SM["kvq"].h, w1)
                        if g >= 3:
                            v.wait_ge(SM["spill"].h, w3)
                        emit_rope(v, qps1, qps2, ctl[p4b], stl[p4b], qqstage[p], rpt,
                                  SM["rope"].h)
                    p1.on("vector", ve_rq)

                    SM["spill"].tick(64)

                    def sy_spq(s, p=p, bt=bt, pos0=pos0, w1=rope_tQ):
                        s.wait_ge(SM["rope"].h, w1)
                        for h in range(2):
                            s.dma_start(out=qscr[h, bt, 0:32, pos0:pos0 + GRP],
                                        in_=qqstage[p][h * 32:(h + 1) * 32, 0:GRP]
                                        ).then_inc(SM["spill"].h, 16)
                            s.dma_start(out=qscr[h, bt, 32:64, pos0:pos0 + GRP],
                                        in_=qqstage[p][h * 32:(h + 1) * 32, GRP:2 * GRP]
                                        ).then_inc(SM["spill"].h, 16)
                    p1.on("gpsimd", sy_spq)

                if pend_v4 is not None:
                    p1.on("scalar", pend_v4)
                p1.emit(nc)

            with ExitStack() as rb2:
                E3 = rb2.enter_context
                OT2 = [E3(nc.sbuf_tensor(f"OT2_{i}", [64, 2 * S], BF16)) for i in range(2)]
                kbuf = [E3(nc.sbuf_tensor(f"kbuf{i}", [128, S], BF16)) for i in range(2)]
                qbuf = [E3(nc.sbuf_tensor(f"qbuf{i}", [128, S], BF16)) for i in range(2)]
                pbuf = [E3(nc.sbuf_tensor(f"pbuf{i}", [128, 2 * QT], BF16)) for i in range(2)]
                lden2 = [E3(nc.sbuf_tensor(f"lden{i}", [NQT, QT], F32))
                         for i in range(2)]
                lseg = E3(nc.sbuf_tensor("lseg", [1, 2 * QT], F32))
                linvh2 = [E3(nc.sbuf_tensor(f"linvh{i}", [NQT, QT], F32))
                          for i in range(2)]
                spair = [E3(nc.psum_tensor(f"spair{i}", [128, 2 * QT], F32)) for i in range(2)]
                obank = [E3(nc.psum_tensor(f"obank{i}", [65, QT], F32)) for i in range(2)]
                lps2 = [E3(nc.psum_tensor(f"lps2{i}", [64, QT], F32)) for i in range(2)]

                p2 = Prog()
                KTS_HB = 2 * NQT * (NQT + 1)

                def mk_ld(hb):
                    h, b = hb // 2, hb % 2
                    w_smm = (KTS_HB // 2) * (hb - 1) if hb >= 2 else 0
                    SM["kq"].tick(32)

                    def ld(s, hb=hb, h=h, b=b, w=w_smm):
                        if hb >= 2:
                            s.wait_ge(SM["smm"].h, w)
                        s.dma_start(out=kbuf[hb % 2][0:64, :], in_=kscr[h, b]
                                    ).then_inc(SM["kq"].h, 16)
                        s.dma_start(out=qbuf[hb % 2][0:64, :], in_=qscr[h, b]
                                    ).then_inc(SM["kq"].h, 16)
                    return ld

                p2.on("sync", mk_ld(0))
                p2.on("sync", mk_ld(1))

                SM["kz"].tick(4)

                def kz_v(v):
                    v.memset(kbuf[0][64:128, :], 0.0).then_inc(SM["kz"].h, 1)
                    v.memset(kbuf[1][64:128, :], 0.0).then_inc(SM["kz"].h, 1)
                p2.on("vector", kz_v)

                def kz_g(g):
                    g.memset(qbuf[0][64:128, :], 0.0).then_inc(SM["kz"].h, 1)
                    g.memset(qbuf[1][64:128, :], 0.0).then_inc(SM["kz"].h, 1)
                p2.on("gpsimd", kz_g)

                pend_items = []
                pend_rh = None
                for hb in range(4):
                    h, b = hb // 2, hb % 2
                    if hb + 2 < 4:
                        p2.on("sync", mk_ld(hb + 2))
                    new_items = []
                    for qt in range(NQT):
                        iq = hb * NQT + qt
                        ktn = 4 * (qt + 1)
                        qtp = qt % 2
                        pvq = []
                        for pj in range(ktn // 2):
                            jj = SM["smm"].n
                            kb, qb = kbuf[hb % 2], qbuf[hb % 2]
                            sb, pb = spair[jj % 2], pbuf[jj % 2]
                            smm_t = SM["smm"].tick()

                            def te_s(te, qt=qt, pj=pj, jj=jj, kb=kb, qb=qb, sb=sb,
                                     w_kq=32 * (hb + 1), hb=hb,
                                     first=(qt == 0 and pj == 0)):
                                if first:
                                    te.wait_ge(SM["kq"].h, w_kq)
                                    if hb == 0:
                                        te.wait_ge(SM["kz"].h, 4)
                                        for i in range(12):
                                            te.matmul(
                                                sb[:, (i % 2) * QT:(i % 2) * QT + QT],
                                                masks_sb[:, (i % 4) * 128:
                                                         (i % 4) * 128 + 128],
                                                masks_sb[:, 0:QT],
                                                start=True, stop=True,
                                                skip_group_check=True)
                                if jj >= 2:
                                    te.wait_ge(SM["exp"].h, jj - 1)
                                for u in range(2):
                                    kt = 2 * pj + u
                                    ins = te.matmul(
                                        sb[:, u * QT:(u + 1) * QT],
                                        kb[:, kt * 128:(kt + 1) * 128],
                                        qb[:, qt * QT:(qt + 1) * QT],
                                        start=True, stop=True,
                                        skip_group_check=True)
                                ins.then_inc(SM["smm"].h, 1)
                            p2.on("tensor", te_s)

                            exp_t = SM["exp"].tick()

                            def sc_e(s, jj=jj, sb=sb, pb=pb, w=smm_t):
                                s.wait_ge(SM["smm"].h, w)
                                if jj >= 2:
                                    s.wait_ge(SM["pv"].h, jj - 1)
                                s.activation(pb[:, :], sb[:, :], AF.Exp,
                                             bias=ebias[:, :], scale=0.125
                                             ).then_inc(SM["exp"].h, 1)
                            p2.on("scalar", sc_e)

                            diag = [(u, 2 * pj + u - 4 * qt) for u in range(2)
                                    if 2 * pj + u >= 4 * qt]
                            if diag:
                                SM["msk"].tick(len(diag))

                                def ve_m(v, pb=pb, diag=diag, w=exp_t):
                                    v.wait_ge(SM["exp"].h, w)
                                    for (u, dx) in diag:
                                        v.tensor_mul(pb[:, u * QT:(u + 1) * QT],
                                                     pb[:, u * QT:(u + 1) * QT],
                                                     masks_sb[:, dx * QT:(dx + 1) * QT]
                                                     ).then_inc(SM["msk"].h, 1)
                                p2.on("vector", ve_m)
                            msk_t = SM["msk"].n
                            pv_t = SM["pv"].tick()

                            def te_pv(te, hb=hb, qt=qt, pj=pj, iq=iq, pb=pb,
                                      qtp=qtp, ktn=ktn, w_e=exp_t, w_m=msk_t,
                                      dg=bool(diag), first=(pj == 0)):
                                if first and iq >= 2:
                                    te.wait_ge(SM["ot"].h, iq - 1)
                                    te.wait_ge(SM["lsh"].h, iq - 1)
                                te.wait_ge(SM["exp"].h, w_e)
                                if dg:
                                    te.wait_ge(SM["msk"].h, w_m)
                                for u in range(2):
                                    kt = 2 * pj + u
                                    base = hb * NKT * 65 + kt * 65
                                    ins = te.matmul(
                                        obank[qtp][:, :], V4[:, base:base + 65],
                                        pb[:, u * QT:(u + 1) * QT],
                                        start=(kt == 0), stop=(kt == ktn - 1),
                                        skip_group_check=True)
                                ins.then_inc(SM["pv"].h, 1)
                            pvq.append(te_pv)
                            if len(pvq) > 2:
                                p2.on("tensor", pvq.pop(0))
                        for fn in pvq:
                            p2.on("tensor", fn)

                        SM["lsh"].tick(1)

                        SM["ot"].tick()

                        def ve_c(v, h=h, b=b, qt=qt, qtp=qtp, iq=iq,
                                 w=SM["pv"].n):
                            if iq >= 2:
                                v.wait_ge(SM["lmv"].h, 16 * (iq - 1))
                            v.wait_ge(SM["pv"].h, w)
                            v.tensor_copy(
                                OT2[h][:, b * S + qt * QT:b * S + (qt + 1) * QT],
                                obank[qtp][0:64, :]).then_inc(SM["ot"].h, 1)
                            v.tensor_copy(lseg[0:1, (qt % 2) * QT:
                                               (qt % 2) * QT + QT],
                                          obank[qtp][64:65, :]
                                          ).then_inc(SM["lsh"].h, 1)
                        p2.on("vector", ve_c)

                        SM["lmv"].tick(16)

                        def g_mv(g, qt=qt, hb=hb, iq=iq, first=(qt == 0)):
                            if first and hb >= 2:
                                g.wait_ge(SM["lr"].h, hb - 1)
                            g.wait_ge(SM["lsh"].h, iq + 1)
                            g.dma_start(out=lden2[hb % 2][qt:qt + 1, :],
                                        in_=lseg[0:1, (qt % 2) * QT:
                                                 (qt % 2) * QT + QT]
                                        ).then_inc(SM["lmv"].h, 16)
                        p2.on("gpsimd", g_mv)

                        gq = iq
                        lrp_t = SM["lrp"].tick()

                        def te_rp2(te, qt=qt, hb=hb, gq=gq):
                            te.wait_ge(SM["lr"].h, hb + 1)
                            if gq >= 2:
                                te.wait_ge(SM["lcp"].h, gq - 1)
                            te.matmul(lps2[qt % 2][:, :],
                                      oh16[0:NQT, qt * 64:(qt + 1) * 64],
                                      linvh2[hb % 2][0:NQT, :],
                                      start=True, stop=True,
                                      skip_group_check=True
                                      ).then_inc(SM["lrp"].h, 1)

                        SM["lcp"].tick()

                        def ve_f(v, h=h, b=b, qt=qt, w=lrp_t):
                            v.wait_ge(SM["lrp"].h, w)
                            v.tensor_mul(
                                OT2[h][:, b * S + qt * QT:b * S + (qt + 1) * QT],
                                OT2[h][:, b * S + qt * QT:b * S + (qt + 1) * QT],
                                lps2[qt % 2][:, :]).then_inc(SM["lcp"].h, 1)

                        new_items.append((te_rp2, ve_f))

                        if pend_rh is not None and qt == NQT // 2:
                            p2.on("vector", pend_rh)
                            pend_rh = None
                        if pend_items and qt > NQT // 2:
                            k = -(-len(pend_items) // (NQT - qt))
                            for _ in range(k):
                                tfn, vfn = pend_items.pop(0)
                                p2.on("tensor", tfn)
                                p2.on("vector", vfn)

                    SM["lr"].tick()

                    def ve_rh(v, hb=hb, w=16 * NQT * (hb + 1)):
                        if hb >= 2:
                            v.wait_ge(SM["lrp"].h, NQT * (hb - 1))
                        v.wait_ge(SM["lmv"].h, w)
                        v.reciprocal(linvh2[hb % 2][:, :], lden2[hb % 2][:, :]
                                     ).then_inc(SM["lr"].h, 1)

                    assert not pend_items
                    pend_items = new_items
                    pend_rh = ve_rh

                if pend_rh is not None:
                    p2.on("vector", pend_rh)
                for tfn, vfn in pend_items:
                    p2.on("tensor", tfn)
                    p2.on("vector", vfn)

                p2.emit(nc)

                p3 = Prog()

                def b3s(s):
                    ogr = og_l.rearrange("(c p) t -> p c t", p=128)
                    s.dma_start(out=ogr[0:64],
                                in_=OT2[0][:, :].rearrange("p (c t) -> p c t", c=n_cores)
                                ).then_inc(SM["og"].h, 16)
                    s.dma_start(out=ogr[64:128],
                                in_=OT2[1][:, :].rearrange("p (c t) -> p c t", c=n_cores)
                                ).then_inc(SM["og"].h, 16)
                p3.on("sync", b3s)

                def b3g(g):
                    g.wait_ge(SM["og"].h, 32)
                    with allow_ici_all_to_all():
                        g.collective_compute(
                            "AllToAll", ALU.bypass,
                            replica_groups=[list(range(n_cores))],
                            ins=[og_l.opt()], outs=[og_m.opt()],
                        ).then_inc(SM["cc"].h, 1)
                    g.wait_ge(SM["cc"].h, 1)
                p3.on("gpsimd", b3g)
                p3.emit(nc)

        xnq = E(nc.sbuf_tensor("xnq", [128, 8 * NT], BF16))
        accum = E(nc.sbuf_tensor("accum", [128, 8 * NT], F32))
        gsb = E(nc.sbuf_tensor("gsb", [8, NT], F32))

        with ExitStack() as rb4:
            E4 = rb4.enter_context
            otf = E4(nc.sbuf_tensor("otf", [128, 8 * NT], BF16))
            wosb = E4(nc.sbuf_tensor("wosb", [128, 8 * D], BF16))
            rtsb = E4(nc.sbuf_tensor("rtsb", [128, 8 * 8], BF16))
            xm = [E4(nc.sbuf_tensor("xm0", [128, 4 * D], F32))]
            xn4 = [E4(nc.sbuf_tensor(f"xn4{i}", [128, D], BF16)) for i in range(4)]
            xsq4 = [E4(nc.sbuf_tensor(f"xsq4{i}", [128, D], F32)) for i in range(4)]
            st4 = E4(nc.sbuf_tensor("st4", [128, 64], F32))
            gtmp = E4(nc.sbuf_tensor("gtmp", [128, 96], F32))
            wops = [E4(nc.psum_tensor(f"wops{i}", [128, QT], F32)) for i in range(2)]
            xps4 = [E4(nc.psum_tensor(f"xps4{i}", [128, 8 * 128], BF16)) for i in range(3)]
            lgps = [E4(nc.psum_tensor(f"lgps{i}", [128, 8], F32)) for i in range(2)]
            gtps = E4(nc.psum_tensor("gtps", [128, 128], F32))

            p4 = Prog()

            def b4_l(s):
                s.dma_start(out=otf[:, :].rearrange("p (c t) -> p c t", c=8),
                            in_=og_m.rearrange("(c p) t -> p c t", p=128)
                            ).then_inc(SM["otf"].h, 16)
            p4.on("sync", b4_l)

            def b4_g(g):
                g.dma_start(out=wosb[:, :].rearrange("p (c o) -> p c o", c=8),
                            in_=wo_t.rearrange("(c p) o -> p c o", p=128)
                            ).then_inc(SM["wocast"].h, 16)
                g.dma_start(out=rtsb[:, :].rearrange("p (c e) -> p c e", c=8),
                            in_=rt_in.rearrange("(c p) e -> p c e", p=128)
                            ).then_inc(SM["rtc"].h, 16)
            p4.on("gpsimd", b4_g)

            for ob in range(8):
                for q in range(NTQ):
                    iw = ob * NTQ + q
                    womm_t = SM["womm"].tick()

                    def te_wo(te, ob=ob, q=q, iw=iw):
                        if iw == 0:
                            te.wait_ge(SM["otf"].h, 16)
                            te.wait_ge(SM["wocast"].h, 16)
                        if iw >= 2:
                            te.wait_ge(SM["woev"].h, iw - 1)
                        for k in range(8):
                            ins = te.matmul(
                                wops[iw % 2][:, :],
                                wosb[:, k * D + ob * 128:k * D + (ob + 1) * 128],
                                otf[:, k * NT + q * QT:k * NT + (q + 1) * QT],
                                start=(k == 0), stop=(k == 7))
                        ins.then_inc(SM["womm"].h, 1)
                    p4.on("tensor", te_wo)

                    SM["woev"].tick()

                    def sc_wo(s, ob=ob, q=q, iw=iw, w=womm_t):
                        s.wait_ge(SM["womm"].h, w)
                        s.copy(accum[:, ob * NT + q * QT:ob * NT + (q + 1) * QT],
                               wops[iw % 2][:, :]).then_inc(SM["woev"].h, 1)
                    p4.on("scalar", sc_wo)

            for gg in range(NX4):
                w_xn = 4 * G + 4 * gg
                x_t = SM["xin"].tick(16)

                def b4x(s, gg=gg, w_xn=w_xn):
                    if gg >= 1:
                        s.wait_ge(SM["xn"].h, w_xn)
                    s.dma_start(
                        out=xm[0][:, :].rearrange("p (t d) -> p t d", t=4),
                        in_=x_mine[gg * GRP:(gg + 1) * GRP, :]
                        .rearrange("(t p) d -> p t d", p=128)
                    ).then_inc(SM["xin"].h, 16)
                p4.on("sync", b4x)

                for t in range(4):
                    emit_ln_sq(p4, SM, xm[0], t, 4 * G + 4 * gg + t, xsq4, x_t,
                               st4)
                for tp in range(2):
                    emit_ln_stats_pair(p4, SM, xm[0], 2 * tp, 2 * tp + 1,
                                       4 * G + 4 * gg + 2 * tp,
                                       4 * G + 4 * gg + 2 * tp + 1,
                                       xsq4, st4, x_t)
                nbts4 = []
                for t in range(4):
                    emit_ln_nb(p4, SM, 4 * G + 4 * gg + t, st4)
                    nbts4.append(SM["vnb"].n)
                for t in range(4):
                    i = 4 * G + 4 * gg + t
                    emit_ln_fin(p4, SM, xm[0], t, i, st4, xn4, nbts4[t])
                for t in range(4):
                    i = 4 * G + 4 * gg + t
                    emit_tr_evac(p4, SM, xn4, xps4, identb, i, xnq,
                                 gg * GRP + t * 128, kvq_guard=None)

            w_xnq = SM["xnTe"].n
            for tbp in range(NTT // 2):
                tbs = (2 * tbp, 2 * tbp + 1)
                lgt_B = None
                for tb in tbs:
                    lgt_B = SM["lgt"].tick()

                    def te_lg(te, tb=tb, w=w_xnq):
                        if tb == 0:
                            te.wait_ge(SM["xnTe"].h, w)
                            te.wait_ge(SM["rtc"].h, 16)
                        if tb >= 2:
                            te.wait_ge(SM["rtv"].h, tb - 1)
                        for k in range(8):
                            ins = te.matmul(lgps[tb % 2][:, :],
                                            xnq[:, k * NT + tb * 128:k * NT + (tb + 1) * 128],
                                            rtsb[:, k * 8:(k + 1) * 8],
                                            start=(k == 0), stop=(k == 7),
                                            skip_group_check=True)
                        ins.then_inc(SM["lgt"].h, 1)
                    p4.on("tensor", te_lg)

                rte_prev = SM["rte"].n
                SM["rtv"].tick()
                rtv_B = SM["rtv"].tick()

                def ve_r2(v, tbs=tbs, w=lgt_B, w_e=rte_prev):
                    v.wait_ge(SM["lgt"].h, w)
                    if w_e >= 1:
                        v.wait_ge(SM["rte"].h, w_e)
                    zz = [lgps[tb % 2][:, :] for tb in tbs]
                    for z, cb in zip(zz, (0, 48)):
                        v.reduce_max(gtmp[:, cb + 0:cb + 1], z, axis=AX.X)
                    for z, cb in zip(zz, (0, 48)):
                        v.tensor_scalar(gtmp[:, cb + 1:cb + 9], z,
                                        gtmp[:, cb + 0:cb + 1], None, ALU.is_ge)
                    for z, cb in zip(zz, (0, 48)):
                        v.scalar_tensor_tensor(gtmp[:, cb + 13:cb + 21],
                                               gtmp[:, cb + 1:cb + 9], -30000.0,
                                               z, ALU.mult, ALU.add)
                    for cb in (0, 48):
                        v.reduce_max(gtmp[:, cb + 9:cb + 10],
                                     gtmp[:, cb + 13:cb + 21], axis=AX.X)
                    for cb in (0, 48):
                        v.tensor_sub(gtmp[:, cb + 45:cb + 46],
                                     gtmp[:, cb + 9:cb + 10],
                                     gtmp[:, cb + 0:cb + 1]
                                     ).then_inc(SM["rtv"].h, 1)
                p4.on("vector", ve_r2)

                SM["rte"].tick()
                rte_B = SM["rte"].tick()

                def sc_r(s, w=rtv_B):
                    s.wait_ge(SM["rtv"].h, w)
                    s.activation(gtmp[:, 10:11], gtmp[:, 45:46], AF.Exp
                                 ).then_inc(SM["rte"].h, 1)
                    s.activation(gtmp[:, 58:59], gtmp[:, 93:94], AF.Exp
                                 ).then_inc(SM["rte"].h, 1)
                p4.on("scalar", sc_r)

                SM["gG"].tick()
                gG_B = SM["gG"].tick()

                def ve_r4(v, w=rte_B):
                    v.wait_ge(SM["rte"].h, w)
                    c = gtmp
                    for cb in (0, 48):
                        v.tensor_scalar(c[:, cb + 46:cb + 47], c[:, cb + 10:cb + 11],
                                        1.0, None, ALU.add)
                    for cb in (0, 48):
                        v.reciprocal(c[:, cb + 11:cb + 12], c[:, cb + 46:cb + 47])
                    for cb in (0, 48):
                        v.tensor_scalar(c[:, cb + 12:cb + 13], c[:, cb + 11:cb + 12],
                                        -1.0, 1.0, ALU.mult, ALU.add)
                    for cb in (0, 48):
                        v.tensor_scalar(c[:, cb + 21:cb + 29], c[:, cb + 13:cb + 21],
                                        c[:, cb + 9:cb + 10], None, ALU.is_ge)
                    for cb in (0, 48):
                        v.tensor_scalar(c[:, cb + 29:cb + 37], c[:, cb + 1:cb + 9],
                                        c[:, cb + 11:cb + 12], None, ALU.mult)
                    for cb in (0, 48):
                        v.scalar_tensor_tensor(c[:, cb + 37:cb + 45],
                                               c[:, cb + 21:cb + 29],
                                               c[:, cb + 12:cb + 13],
                                               c[:, cb + 29:cb + 37],
                                               ALU.mult, ALU.add
                                               ).then_inc(SM["gG"].h, 1)
                p4.on("vector", ve_r4)

                for (tb, cb) in zip(tbs, (0, 48)):
                    gev_prev = SM["gev"].n
                    gtr_t = SM["gtr"].tick()

                    def te_gt(te, cb=cb, w=gG_B, wc=gev_prev):
                        te.wait_ge(SM["gG"].h, w)
                        if wc >= 1:
                            te.wait_ge(SM["gev"].h, wc)
                        te.transpose(gtps[0:8, 0:128], gtmp[:, cb + 37:cb + 45],
                                     identf[:, :]).then_inc(SM["gtr"].h, 1)
                    p4.on("tensor", te_gt)

                    SM["gev"].tick()

                    def sc_gt(s, tb=tb, w=gtr_t):
                        s.wait_ge(SM["gtr"].h, w)
                        s.copy(gsb[0:8, tb * 128:(tb + 1) * 128], gtps[0:8, 0:128]
                               ).then_inc(SM["gev"].h, 1)
                    p4.on("scalar", sc_gt)

            p4.emit(nc)

        if debug:
            pD = Prog()
            b0w = SM["b0"].tick(64)

            def bdg(s, w=b0w):
                s.dma_start(out=dbg_acc, in_=accum[:, :]).then_inc(SM["b0"].h, 16)
                s.dma_start(out=dbg_gsb, in_=gsb[0:8, :]).then_inc(SM["b0"].h, 16)
                s.dma_start(out=dbg_xnq, in_=xnq[:, :]).then_inc(SM["b0"].h, 16)
                s.dma_start(out=dbg_ogm, in_=og_m).then_inc(SM["b0"].h, 16)
                s.wait_ge(SM["b0"].h, w)
            pD.on("sync", bdg)
            pD.emit(nc)

        with ExitStack() as rb5:
            E5 = rb5.enter_context
            hsq = E5(nc.sbuf_tensor("hsq", [128, 16 * NT], BF16))
            gsr2 = E5(nc.sbuf_tensor("gsr2", [1, 2 * NT], F32))
            wfcsb = [E5(nc.sbuf_tensor(f"wfc{i}", [128, 8 * 256], BF16)) for i in range(2)]
            wprsb = [E5(nc.sbuf_tensor(f"wpr{i}", [128, 4 * 128], BF16)) for i in range(2)]
            g128 = [E5(nc.sbuf_tensor(f"g128_{i}", [128, QT], F32)) for i in range(NTQ)]
            rbuf = [E5(nc.sbuf_tensor(f"rbuf{i}", [128, QT], BF16)) for i in range(2)]
            yps = [E5(nc.psum_tensor(f"yps{i}", [128, QT], F32)) for i in range(8)]
            hps = [yps[0], yps[1]]
            gqp = yps[2]

            wfc_guard = [0, 0]
            wpr_guard = [0, 0]

            p5 = Prog()
            for e in range(NEXP):
                gated = e < 8
                yev_prev = SM["yev"].n
                sqd_prev_e = SM["sqd"].n

                if gated:
                    grp_prev = NTQ * (e - 1) if e >= 2 else 0
                    gext_t = SM["grep"].tick(16)

                    def sc_gx(s, e=e, w=grp_prev):
                        if w > 0:
                            s.wait_ge(SM["grp"].h, w)
                        s.dma_start(out=gsr2[0:1, (e % 2) * NT:(e % 2 + 1) * NT],
                                    in_=gsb[e:e + 1, :]).then_inc(SM["grep"].h, 16)
                    p5.on("scalar", sc_gx)

                    for q in range(NTQ):
                        gcp_prev = SM["gcp"].n
                        grp_t = SM["grp"].tick()

                        def te_gq(te, e=e, q=q, w_yev=yev_prev, wc=gcp_prev,
                                  w_g=gext_t):
                            if q == 0:
                                te.wait_ge(SM["yev"].h, w_yev)
                                te.wait_ge(SM["grep"].h, w_g)
                            if wc >= 1:
                                te.wait_ge(SM["gcp"].h, wc)
                            te.matmul(gqp[:, :], ones_f[0:1, 0:128],
                                      gsr2[0:1, (e % 2) * NT + q * QT:
                                           (e % 2) * NT + (q + 1) * QT],
                                      start=True, stop=True, skip_group_check=True
                                      ).then_inc(SM["grp"].h, 1)
                        p5.on("tensor", te_gq)

                        gcp_t = SM["gcp"].tick()

                        def sc_gq(s, q=q, w=grp_t, wx=sqd_prev_e):
                            s.wait_ge(SM["grp"].h, w)
                            if wx >= 1:
                                s.wait_ge(SM["sqd"].h, wx)
                            s.copy(g128[q][:, :], gqp[:, :]).then_inc(SM["gcp"].h, 1)
                        p5.on("scalar", sc_gq)
                gcp_all = SM["gcp"].n

                for dh in range(2):
                    ph2_prev_ymm = SM["ymm"].n
                    pend = [None]
                    for u2 in range(16 * NTQ):
                        cp = u2 // (2 * NTQ)
                        u = (u2 // NTQ) % 2
                        q = u2 % NTQ
                        cl = cp * 2 + u
                        if u == 0 and q == 0:
                            w_buf = wfc_guard[cp % 2]
                            wfcd_t = SM["wfcd"].tick(16)

                            def gp_fc(g, e=e, dh=dh, cp=cp, w_h=w_buf):
                                if w_h >= 1:
                                    g.wait_ge(SM["hmm"].h, w_h)
                                c0 = (dh * 16 + cp * 2) * 128
                                g.dma_start(
                                    out=wfcsb[cp % 2][:, :].rearrange(
                                        "p (k f) -> p k f", k=8),
                                    in_=wfc_t[e][:, c0:c0 + 256]
                                    .rearrange("(k p) f -> p k f", p=128)
                                ).then_inc(SM["wfcd"].h, 16)
                            p5.on("gpsimd", gp_fc)

                        hc = SM["hmm"].n
                        hmm_t = SM["hmm"].tick()

                        def te_h(te, q=q, u=u, hc=hc, cp=cp, w_d=wfcd_t,
                                 e=e, gated=gated,
                                 first=(dh == 0 and cp == 0 and u == 0 and q == 0),
                                 w_yv=yev_prev):
                            te.wait_ge(SM["wfcd"].h, w_d)
                            if (not gated) and first:
                                te.wait_ge(SM["yev"].h, w_yv)
                            if hc >= 2:
                                te.wait_ge(SM["relu"].h, hc - 1)
                            for k in range(8):
                                ins = te.matmul(
                                    hps[hc % 2][:, :],
                                    wfcsb[cp % 2][:, k * 256 + u * 128:
                                                  k * 256 + (u + 1) * 128],
                                    xnq[:, k * NT + q * QT:
                                        k * NT + (q + 1) * QT],
                                    start=(k == 0), stop=(k == 7),
                                    skip_group_check=True)
                            ins.then_inc(SM["hmm"].h, 1)
                        p5.on("tensor", te_h)

                        sqv_free = SM["sqv"].n - 1
                        relu_t = SM["relu"].tick()

                        def sc_h(s, hc=hc, w=hmm_t, w_s=sqv_free):
                            s.wait_ge(SM["hmm"].h, w)
                            if hc >= 2:
                                s.wait_ge(SM["sqv"].h, w_s)
                            s.activation(rbuf[hc % 2][:, :], hps[hc % 2][:, :],
                                         AF.Relu).then_inc(SM["relu"].h, 1)
                        p5.on("scalar", sc_h)

                        prev = pend[0]
                        pend[0] = (cl, q)
                        SM["sqv"].tick()
                        if gated and prev is not None:
                            SM["sqd"].tick()

                        def ve_h(v, cl=cl, q=q, hc=hc, w=relu_t,
                                 w_y=ph2_prev_ymm, gated=gated, w_g=gcp_all,
                                 prev=prev):
                            v.wait_ge(SM["relu"].h, w)
                            if w_y > 0 and cl == 0 and q == 0:
                                v.wait_ge(SM["ymm"].h, w_y)
                            if gated and cl == 0 and q == 0 and w_g >= 1:
                                v.wait_ge(SM["gcp"].h, w_g)
                            dst = hsq[:, cl * NT + q * QT:cl * NT + (q + 1) * QT]
                            v.tensor_mul(dst, rbuf[hc % 2][:, :],
                                         rbuf[hc % 2][:, :]).then_inc(SM["sqv"].h, 1)
                            if gated and prev is not None:
                                pcl, pq = prev
                                pdst = hsq[:, pcl * NT + pq * QT:
                                           pcl * NT + (pq + 1) * QT]
                                v.tensor_mul(pdst, pdst, g128[pq][:, :]
                                             ).then_inc(SM["sqd"].h, 1)
                        p5.on("vector", ve_h)
                        if u == 1 and q == NTQ - 1:
                            wfc_guard[cp % 2] = SM["hmm"].n

                    if gated:
                        SM["sqd"].tick()

                        def ve_fl(v, prev=pend[0]):
                            pcl, pq = prev
                            pdst = hsq[:, pcl * NT + pq * QT:pcl * NT + (pq + 1) * QT]
                            v.engine_nop()
                            v.tensor_mul(pdst, pdst, g128[pq][:, :]
                                         ).then_inc(SM["sqd"].h, 1)
                        p5.on("vector", ve_fl)

                    sq_half_done = SM["sqd"].n if gated else SM["sqv"].n
                    for ob in range(8):
                        dbg = dh * 8 + ob
                        bs = (dbg % 2) * 4
                        yev_need = SM["yev"].n - 1
                        for cq in range(4):
                            w_buf = wpr_guard[cq % 2]
                            wprd_t = SM["wprd"].tick(16)

                            def gp_pr(g, e=e, dh=dh, ob=ob, cq=cq, w_y=w_buf):
                                if w_y >= 1:
                                    g.wait_ge(SM["ymm"].h, w_y)
                                r0 = dh * 2048 + cq * 512
                                g.dma_start(
                                    out=wprsb[cq % 2][:, :].rearrange(
                                        "p (u o) -> p u o", u=4),
                                    in_=wpr_t[e][r0:r0 + 512, ob * 128:(ob + 1) * 128]
                                    .rearrange("(u p) o -> p u o", p=128)
                                ).then_inc(SM["wprd"].h, 16)
                            p5.on("gpsimd", gp_pr)

                            for u in range(4):
                                cc = cq * 4 + u
                                ymm_t = SM["ymm"].tick()

                                def te_y(te, cc=cc, cq=cq, u=u, bs=bs, w_d=wprd_t,
                                         w_s=sq_half_done, w_e=yev_need,
                                         sq_sem=("sqd" if gated else "sqv"),
                                         first=(cc == 0)):
                                    te.wait_ge(SM["wprd"].h, w_d)
                                    if first:
                                        te.wait_ge(SM[sq_sem].h, w_s)
                                        if w_e >= 1:
                                            te.wait_ge(SM["yev"].h, w_e)
                                    for q in range(NTQ):
                                        ins = te.matmul(
                                            yps[bs + q][:, :],
                                            wprsb[cq % 2][:, u * 128:(u + 1) * 128],
                                            hsq[:, cc * NT + q * QT:cc * NT + (q + 1) * QT],
                                            start=(cc == 0), stop=(cc == 15),
                                            skip_group_check=True)
                                    ins.then_inc(SM["ymm"].h, 1)
                                p5.on("tensor", te_y)
                            wpr_guard[cq % 2] = SM["ymm"].n

                        ymm_all = SM["ymm"].n
                        SM["yev"].tick()

                        def ve_y(v, ob=ob, bs=bs, w=ymm_all):
                            v.wait_ge(SM["ymm"].h, w)
                            for q in range(NTQ):
                                ins = v.scalar_tensor_tensor(
                                    accum[:, ob * NT + q * QT:ob * NT + (q + 1) * QT],
                                    yps[bs + q][:, :], 1.0,
                                    accum[:, ob * NT + q * QT:ob * NT + (q + 1) * QT],
                                    ALU.mult, ALU.add)
                            ins.then_inc(SM["yev"].h, 1)
                        p5.on("vector", ve_y)

            p5.emit(nc)

        with ExitStack() as rb6:
            E6 = rb6.enter_context
            xr = [E6(nc.sbuf_tensor(f"xr{i}", [128, 4 * D], F32)) for i in range(2)]
            osb = [E6(nc.sbuf_tensor(f"osb{i}", [128, D], F32)) for i in range(2)]
            eps = [E6(nc.psum_tensor(f"eps{i}", [128, 1024], F32)) for i in range(2)]

            p6 = Prog()
            for tt in range(NTT):
                if tt % 4 == 0:
                    gg = tt // 4
                    epv_w = 4 * gg - 4
                    SM["epx"].tick(16)

                    def b6x(s, gg=gg, w=epv_w):
                        if gg >= 2:
                            s.wait_ge(SM["epv"].h, w)
                        s.dma_start(
                            out=xr[gg % 2][:, :].rearrange("p (t d) -> p t d", t=4),
                            in_=x_mine[gg * GRP:(gg + 1) * GRP, :]
                            .rearrange("(t p) d -> p t d", p=128)
                        ).then_inc(SM["epx"].h, 16)
                    p6.on("sync", b6x)

                ept_t = SM["ept"].tick()

                def te_ep(te, tt=tt):
                    if tt >= 2:
                        te.wait_ge(SM["epv"].h, tt - 1)
                    for k in range(8):
                        ins = te.transpose(eps[tt % 2][:, k * 128:(k + 1) * 128],
                                           accum[:, k * NT + tt * 128:
                                                 k * NT + (tt + 1) * 128],
                                           identf[:, :])
                    ins.then_inc(SM["ept"].h, 1)
                p6.on("tensor", te_ep)

                epx_w = 16 * (tt // 4 + 1)
                out_w = 16 * (tt - 1)
                SM["epv"].tick()

                def ve_ep(v, tt=tt, w_t=ept_t, w_x=epx_w, w_o=out_w):
                    v.wait_ge(SM["ept"].h, w_t)
                    v.wait_ge(SM["epx"].h, w_x)
                    if tt >= 2:
                        v.wait_ge(SM["out"].h, w_o)
                    v.tensor_add(osb[tt % 2][:, :],
                                 xr[(tt // 4) % 2][:, (tt % 4) * D:(tt % 4 + 1) * D],
                                 eps[tt % 2][:, :]).then_inc(SM["epv"].h, 1)
                p6.on("vector", ve_ep)

                epv_t = SM["epv"].n
                SM["out"].tick(16)

                def b6o(s, tt=tt, w=epv_t):
                    s.wait_ge(SM["epv"].h, w)
                    s.dma_start(out=out_mine[tt * 128:(tt + 1) * 128, :],
                                in_=osb[tt % 2][:, :]).then_inc(SM["out"].h, 16)
                p6.on("sync", b6o)

            out_all = SM["out"].n

            def b6f(s, w=out_all):
                s.wait_ge(SM["out"].h, w)
            p6.on("sync", b6f)
            p6.emit(nc)

    return nc


def emit_ln_sq(prog, SM, xsrc, t, i, xsq, xin_target, stats):
    sl = (i % 8) * 8
    def sc_sq(s, t=t, i=i, sl=sl, w_x=xin_target):
        if t == 0:
            s.wait_ge(SM["xin"].h, w_x)
        s.activation(xsq[i % 4][:, :], xsrc[:, t * D:(t + 1) * D], AF.Square,
                     accum_out=stats[:, sl + 1:sl + 2]
                     ).then_inc(SM["sq"].h, 1)
    prog.on("scalar", sc_sq)
    SM["sq"].tick()


def emit_ln_stats_pair(prog, SM, xsrc, tA, tB, iA, iB, xsq, stats, xin_target):
    slA = (iA % 8) * 8
    slB = (iB % 8) * 8
    vstat_tB = SM["vstat"].tick() + 1
    SM["vstat"].tick()

    def ve_st(v, tA=tA, tB=tB, iA=iA, iB=iB, slA=slA, slB=slB, w_x=xin_target):
        xtA = xsrc[:, tA * D:(tA + 1) * D]
        xtB = xsrc[:, tB * D:(tB + 1) * D]
        if tA == 0:
            v.wait_ge(SM["xin"].h, w_x)
        v.reduce_sum(stats[:, slA:slA + 1], xtA, axis=AX.X)
        v.reduce_sum(stats[:, slB:slB + 1], xtB, axis=AX.X)
        v.wait_ge(SM["sq"].h, iB + 1)
        for sl in (slA, slB):
            v.tensor_scalar(stats[:, sl + 2:sl + 3], stats[:, sl:sl + 1],
                            1.0 / D, None, ALU.mult)
        for sl in (slA, slB):
            v.tensor_scalar(stats[:, sl + 3:sl + 4], stats[:, sl + 1:sl + 2],
                            1.0 / D, 1e-5, ALU.mult, ALU.add)
        for sl in (slA, slB):
            m = stats[:, sl + 2:sl + 3]
            v.tensor_mul(stats[:, sl + 4:sl + 5], m, m)
        for sl in (slA, slB):
            ins = v.scalar_tensor_tensor(stats[:, sl + 5:sl + 6],
                                         stats[:, sl + 3:sl + 4], 1.0,
                                         stats[:, sl + 4:sl + 5],
                                         ALU.mult, ALU.subtract)
            ins.then_inc(SM["vstat"].h, 1)
    prog.on("vector", ve_st)

    SM["sln"].tick()
    SM["sln"].tick()

    def sc_ln(s, slA=slA, slB=slB, w=vstat_tB):
        s.wait_ge(SM["vstat"].h, w)
        s.activation(stats[:, slA + 6:slA + 7], stats[:, slA + 5:slA + 6], AF.Ln)
        s.activation(stats[:, slB + 6:slB + 7], stats[:, slB + 5:slB + 6], AF.Ln)
        s.activation(stats[:, slA + 6:slA + 7], stats[:, slA + 6:slA + 7],
                     AF.Exp, scale=-0.5).then_inc(SM["sln"].h, 1)
        s.activation(stats[:, slB + 6:slB + 7], stats[:, slB + 6:slB + 7],
                     AF.Exp, scale=-0.5).then_inc(SM["sln"].h, 1)
    prog.on("scalar", sc_ln)


def emit_ln_nb(prog, SM, i, stats):
    sl = (i % 8) * 8
    sln_t = i + 1
    SM["vnb"].tick()

    def ve_nb(v, sl=sl, w=sln_t):
        v.wait_ge(SM["sln"].h, w)
        v.scalar_tensor_tensor(stats[:, sl + 7:sl + 8], stats[:, sl + 2:sl + 3],
                               -1.0, stats[:, sl + 6:sl + 7], ALU.mult, ALU.mult
                               ).then_inc(SM["vnb"].h, 1)
    prog.on("vector", ve_nb)


def emit_ln_fin(prog, SM, xsrc, t, i, stats, xn_out, vnb_t):
    sl = (i % 8) * 8
    ttr_prev = SM["ttr"].n
    SM["xn"].tick()

    def sc_xn(s, t=t, i=i, sl=sl, w=vnb_t, w_t=ttr_prev):
        s.wait_ge(SM["vnb"].h, w)
        if i >= 4:
            s.wait_ge(SM["ttr"].h, w_t - 3)
        s.activation(xn_out[i % 4][:, :], xsrc[:, t * D:(t + 1) * D],
                     AF.Identity, bias=stats[:, sl + 7:sl + 8],
                     scale=stats[:, sl + 6:sl + 7]).then_inc(SM["xn"].h, 1)
    prog.on("scalar", sc_xn)


def emit_tr_evac(prog, SM, xn_bufs, xps_bufs, identb, i, dest, dest_col, kvq_guard):
    xn_t = SM["xn"].n
    xnTe_prev = SM["xnTe"].n
    ttr_t = SM["ttr"].tick()

    def te_tr(te, i=i, w=xn_t, w_e=xnTe_prev):
        te.wait_ge(SM["xn"].h, w)
        if i >= 3:
            te.wait_ge(SM["xnTe"].h, w_e - 2)
        for k in range(8):
            ins = te.transpose(xps_bufs[i % 3][:, k * 128:(k + 1) * 128],
                               xn_bufs[i % 4][:, k * 128:(k + 1) * 128],
                               identb[:, :])
        ins.then_inc(SM["ttr"].h, 1)
    prog.on("tensor", te_tr)

    SM["xnTe"].tick()

    def ve_ev(v, i=i, w=ttr_t, kvq_guard=kvq_guard, dest_col=dest_col):
        v.wait_ge(SM["ttr"].h, w)
        if kvq_guard is not None and kvq_guard > 0:
            v.wait_ge(SM["kvq"].h, kvq_guard)
        v.tensor_copy(dest[:, :].rearrange("p (k s) -> p k s", k=8)
                      [:, :, dest_col:dest_col + 128],
                      xps_bufs[i % 3][:, :].rearrange("p (k s) -> p k s", k=8)
                      ).then_inc(SM["xnTe"].h, 1)
    prog.on("vector", ve_ev)


def emit_rope(v, ps1, ps2, ct, stl, stage, rpt, rope_h):
    v.tensor_mul(rpt[0][:, :], ps1[:, :], ct[:, :])
    v.tensor_mul(rpt[1][:, :], ps2[:, :], stl[:, :])
    v.tensor_mul(rpt[2][:, :], ps1[:, :], stl[:, :])
    v.tensor_mul(rpt[3][:, :], ps2[:, :], ct[:, :])
    v.tensor_sub(stage[:, 0:GRP], rpt[0][:, :], rpt[1][:, :])
    v.tensor_add(stage[:, GRP:2 * GRP], rpt[2][:, :], rpt[3][:, :]
                 ).then_inc(rope_h, 1)


def _host_prep(inputs, seq=8192, n_cores=8):
    S = seq
    NT = 2 * S // n_cores
    X = np.ascontiguousarray(np.asarray(inputs["L0_attn_x"], dtype=np.float32))
    Wq = np.asarray(inputs["L0_attn_Wq"], dtype=np.float32)
    Wk = np.asarray(inputs["L0_attn_Wk"], dtype=np.float32)
    Wv = np.asarray(inputs["L0_attn_Wv"], dtype=np.float32)
    Wo = np.asarray(inputs["L0_attn_Wo"], dtype=np.float32)
    Rw = np.asarray(inputs["L0_ffn_router_w"], dtype=np.float32)
    Wfc = np.asarray(inputs["L0_ffn_w_fc"], dtype=np.float32)
    Wpr = np.asarray(inputs["L0_ffn_w_proj"], dtype=np.float32)
    Sfc = np.asarray(inputs["L0_ffn_shared_fc"], dtype=np.float32)
    Spr = np.asarray(inputs["L0_ffn_shared_proj"], dtype=np.float32)

    wfc_t = np.empty((NEXP, D, DFF), dtype=np.float32)
    wpr_t = np.empty((NEXP, DFF, D), dtype=np.float32)
    for e in range(8):
        wfc_t[e] = Wfc[e].T
        wpr_t[e] = Wpr[e].T
    wfc_t[8] = Sfc.T
    wpr_t[8] = Spr.T

    wo_t = np.ascontiguousarray(Wo.T)
    rt = np.ascontiguousarray(Rw.T)

    inv = 10000.0 ** (-np.arange(32, dtype=np.float64) / 32.0)
    pos = np.arange(S, dtype=np.float64)
    ang = pos[None, :] * inv[:, None]
    ang2 = np.concatenate([ang, ang], axis=0)
    cos_t = np.cos(ang2).astype(np.float32)
    sin_t = np.sin(ang2).astype(np.float32)

    masks = np.zeros((4, 128, QT), dtype=np.float32)
    kk = np.arange(128)[:, None]
    qq = np.arange(QT)[None, :]
    for dx in range(4):
        masks[dx] = (qq >= kk + dx * 128).astype(np.float32)

    ident = np.eye(128, dtype=np.float32)
    ones_r = np.ones((1, 128), dtype=np.float32)
    NQT = S // QT
    onehot = np.zeros((NQT, NQT * 64), dtype=np.float32)
    for q in range(NQT):
        onehot[q, q * 64:(q + 1) * 64] = 1.0

    xflat = X.reshape(2 * S, D)
    in_maps = []
    for c in range(n_cores):
        h0, h1 = 2 * c, 2 * c + 1
        rows = (
            [h0 * 64 + 2 * i for i in range(32)]
            + [h1 * 64 + 2 * i for i in range(32)]
            + [h0 * 64 + 2 * i + 1 for i in range(32)]
            + [h1 * 64 + 2 * i + 1 for i in range(32)]
        )
        vrows = list(range(h0 * 64, h0 * 64 + 64)) + list(range(h1 * 64, h1 * 64 + 64))
        in_maps.append({
            "x_full": X,
            "x_mine": np.ascontiguousarray(xflat[c * NT:(c + 1) * NT]),
            "wq_sl": np.ascontiguousarray(Wq[rows, :].T),
            "wk_sl": np.ascontiguousarray(Wk[rows, :].T),
            "wv_sl": np.ascontiguousarray(Wv[vrows, :].T),
            "wo_t": wo_t,
            "rt": rt,
            "wfc_t": wfc_t,
            "wpr_t": wpr_t,
            "cos_t": cos_t,
            "sin_t": sin_t,
            "masks": masks,
            "onehot": onehot,
            "ident": ident,
            "ones_r": ones_r,
        })
    return in_maps


def kernel(**inputs):
    seq = int(np.asarray(inputs["L0_attn_x"]).shape[1])
    n_cores = 8
    NT = 2 * seq // n_cores
    in_maps = _host_prep(inputs, seq=seq, n_cores=n_cores)
    nc = build_program(seq=seq, n_cores=n_cores)
    res = run_bass_kernel_spmd(nc, in_maps, core_ids=list(range(n_cores)))
    out = np.empty((2 * seq, D), dtype=np.float32)
    for c in range(n_cores):
        out[c * NT:(c + 1) * NT] = res.results[c]["out_mine"]
    return out.reshape(2, seq, D)
